# version 25
# baseline (speedup 1.0000x reference)
import contextlib

import numpy as np
import ml_dtypes

import concourse.bass as bass
import concourse.mybir as mybir
from concourse.bass_utils import run_bass_kernel_spmd

F32 = mybir.dt.float32
BF16 = mybir.dt.bfloat16
AF = mybir.ActivationFunctionType
ALU = mybir.AluOpType
NPBF = ml_dtypes.bfloat16

D = 1024
T = 2048
E = 2048
NC = 8
NBLK = 4
BLK = 512
EPS = 1e-5

ENABLE = (1, 2, 3, 4, 5, 6)

ENGS = ("sync", "gpsimd", "scalar", "vector", "tensor")
SEM_OF = {"sync": "sy", "gpsimd": "sg", "scalar": "ss",
          "vector": "sv", "tensor": "st"}
BSEM_OF = {"sync": "bsy", "gpsimd": "bsg", "scalar": "bss",
           "vector": "bsv", "tensor": "bst"}


class Prog:

    def __init__(self, sems):
        self.ops = {e: [] for e in ENGS}
        self.sems = sems
        self.cnt = {k: 0 for k in sems}

    def emit(self, eng, fn, inc=None, inc_n=1):
        if inc is None:
            self.ops[eng].append(fn)
            return
        h = self.sems[inc]
        self.cnt[inc] += inc_n

        def wrapped(e, fn=fn, h=h, n=inc_n):
            fn(e).then_inc(h, n)
        self.ops[eng].append(wrapped)

    def dma(self, eng, out, in_, inc="qS", slow=False):
        h = self.sems[inc]
        pre = self.cnt[inc]
        self.ops[eng].append(lambda e, h=h, v=pre: e.wait_ge(h, v))
        self.cnt[inc] += 16

        def fn(e, out=out, in_=in_, h=h, slow=slow):
            e.dma_start(out=out, in_=in_,
                        allow_slow_non_contiguous=slow).then_inc(h, 16)
        self.ops[eng].append(fn)

    def val(self, sem):
        return self.cnt[sem]

    def wait(self, eng, sem, val=None):
        v = self.cnt[sem] if val is None else val
        h = self.sems[sem]
        self.ops[eng].append(lambda e, h=h, v=v: e.wait_ge(h, v))

    def drain(self, eng):
        self.ops[eng].append(lambda e: e.drain())

    def fence(self, eng):
        self.wait(eng, SEM_OF[eng])

    def bank_wait(self, i):
        if i in getattr(self, "bank_last", {}):
            sem, v = self.bank_last[i]
            self.wait("tensor", sem, v)

    def bank_set(self, i, sem):
        if not hasattr(self, "bank_last"):
            self.bank_last = {}
        self.bank_last[i] = (sem, self.val(sem))

    def barrier(self, engines=ENGS):
        for e in engines:
            s = BSEM_OF[e]
            h = self.sems[s]
            self.cnt[s] += 1
            self.ops[e].append(lambda en, h=h: en.sem_inc(h, 1))
        for e in engines:
            for o in engines:
                if o != e:
                    self.wait(e, BSEM_OF[o])


def build_program():
    nc = bass.Bass("TRN2", target_bir_lowering=False)

    def din(name, shape, dt=F32):
        return nc.dram_tensor(name, shape, dt, kind="ExternalInput").ap()

    x_in = din("x_in", [D, T])
    out_t = nc.dram_tensor("out", [D, T], F32, kind="ExternalOutput").ap()

    CC = din("CC", [D, D], BF16)
    SS = din("SS", [D, D], BF16)
    FF_Wr = din("FF_Wr", [D, D])
    FF_Wi = din("FF_Wi", [D, D])
    FF_bias = din("FF_bias", [D])
    L0_inwT = din("L0_inwT", [D, E])
    L0_skipT = din("L0_skipT", [D, E])
    L0_outwT = din("L0_outwT", [E, D])
    L0_d1bcT = din("L0_d1bcT", [E, 96])
    L0_wd2T = din("L0_wd2T", [64, E])
    L0_Alog = din("L0_Alog", [E, 16])
    L0_convw = din("L0_convw", [E, 4])
    L0_convb = din("L0_convb", [E])
    L0_wd2b = din("L0_wd2b", [E])
    L0_WD = din("L0_WD", [E])
    L0_normw = din("L0_normw", [D])
    Q_wqkv = din("Q_wqkv", [D, 3 * D])
    Q_bqkv = din("Q_bqkv", [3 * D])
    Q_wproj = din("Q_wproj", [D, D])
    Q_bproj = din("Q_bproj", [D])
    Q_btab = din("Q_btab", [225, 16])
    Q_OH = din("Q_OH", [225, 64], BF16)
    E8c = din("E8c", [8, 128], BF16)
    E16w = din("E16w", [16, 128], BF16)
    BKw = din("BKw", [16, 128], BF16)
    IDN = din("IDN", [128, 128], BF16)
    W1_WrT = din("W1_WrT", [D, D])
    W1_WkT = din("W1_WkT", [D, D])
    W1_WvT = din("W1_WvT", [D, D])
    W1_mur = din("W1_mur", [D])
    SELR = din("SELR", [16, 16, 128], BF16)
    W1_muk = din("W1_muk", [D])

    scr_x = nc.dram_tensor("scr_x", [D, T], F32, kind="Internal").ap()
    scr_x1 = nc.dram_tensor("scr_x1", [E, T], BF16, kind="Internal").ap()
    scr_yg = nc.dram_tensor("scr_yg", [E, T], BF16, kind="Internal").ap()
    scr_M = nc.dram_tensor("scr_M", [128, 16384], BF16, kind="Internal").ap()
    scr_sm = nc.dram_tensor("scr_sm", [64, 16], BF16, kind="Internal").ap()
    scr_po = nc.dram_tensor("scr_po", [D, T], F32, kind="Internal").ap()

    x_in3 = x_in.rearrange("(c p) t -> p c t", p=128)
    scr_x3 = scr_x.rearrange("(c p) t -> p c t", p=128)
    out3 = out_t.rearrange("(c p) t -> p c t", p=128)
    scr_po3 = scr_po.rearrange("(c p) t -> p c t", p=128)

    with contextlib.ExitStack() as st_:
        EE = st_.enter_context

        AR_N = 24576
        arena = EE(nc.sbuf_tensor("arena", [128, AR_N], F32))

        def arf(off, n):
            assert off >= 0 and off + n <= AR_N, (off, n)
            return arena[:, off:off + n]

        def arb(off, n):
            assert off >= 0 and off + n <= AR_N, (off, n)
            return arena[:, off:off + n].bitcast(BF16)

        xnbf = EE(nc.sbuf_tensor("xnbf", [128, 8, T], BF16))
        xblk = EE(nc.sbuf_tensor("xblk", [128, 8, BLK], F32))
        tmp512 = EE(nc.sbuf_tensor("tmp512", [128, BLK], F32))
        b1f = EE(nc.sbuf_tensor("b1f", [128, BLK], F32))
        tmp2 = EE(nc.sbuf_tensor("tmp2", [128, 2, BLK], F32))
        b2f = EE(nc.sbuf_tensor("b2f", [128, BLK], F32))
        rows = EE(nc.sbuf_tensor("rows", [1, 2, T], F32))
        st16 = EE(nc.sbuf_tensor("st16", [128, 12, 16], F32))
        ones_b = EE(nc.sbuf_tensor("ones_b", [128, 1], BF16))
        ones_f1 = EE(nc.sbuf_tensor("ones_f1", [1, 128], F32))
        onesb_1 = EE(nc.sbuf_tensor("onesb_1", [1, 128], BF16))
        nwrow = EE(nc.sbuf_tensor("nwrow", [1, D], F32))
        vecs = EE(nc.sbuf_tensor("vecs", [128, 64], F32))
        Aneg = EE(nc.sbuf_tensor("Aneg", [128, 16, 16], F32))
        qb_sc = EE(nc.sbuf_tensor("qb_sc", [128, 24], F32))
        pbias = EE(nc.sbuf_tensor("pbias", [128, 8], F32))
        murc = EE(nc.sbuf_tensor("murc", [128, 8], F32))
        mukc = EE(nc.sbuf_tensor("mukc", [128, 8], F32))
        e8_sb = EE(nc.sbuf_tensor("e8_sb", [8, 128], BF16))
        e16_sb = EE(nc.sbuf_tensor("e16_sb", [16, 128], BF16))
        bkw_sb = EE(nc.sbuf_tensor("bkw_sb", [16, 128], BF16))
        idn_sb = EE(nc.sbuf_tensor("idn_sb", [128, 128], BF16))
        oh_a = EE(nc.sbuf_tensor("oh_a", [128, 64], BF16))
        oh_b = EE(nc.sbuf_tensor("oh_b", [97, 64], BF16))
        bt_a = EE(nc.sbuf_tensor("bt_a", [128, 16], BF16))
        bt_b = EE(nc.sbuf_tensor("bt_b", [97, 16], BF16))
        bt_af = EE(nc.sbuf_tensor("bt_af", [128, 16], F32))
        bt_bf = EE(nc.sbuf_tensor("bt_bf", [97, 16], F32))
        bias8 = EE(nc.sbuf_tensor("bias8", [64, 16], BF16))
        bcolb = EE(nc.sbuf_tensor("bcolb", [8, 8, 16], BF16))
        bcol = EE(nc.sbuf_tensor("bcol", [8, 16, 16, 8], BF16))
        convw_sb = EE(nc.sbuf_tensor("convw_sb", [128, 16, 4], F32))
        d1bc_sb = EE(nc.sbuf_tensor("d1bc_sb", [128, 16, 96], BF16))
        wd2_sb = EE(nc.sbuf_tensor("wd2_sb", [64, 16, 128], BF16))
        hcar = EE(nc.sbuf_tensor("hcar", [128, 16, 16], F32))
        csts = EE(nc.sbuf_tensor("csts", [128, 4], F32))
        qbv_row = EE(nc.sbuf_tensor("qbv_row", [1, D], BF16))
        PbfB = [EE(nc.sbuf_tensor("PbfA", [128, 128], F32)),
                EE(nc.sbuf_tensor("PbfBt", [128, 128], F32))]
        PbfPB = [EE(nc.sbuf_tensor("PbfPA", [128, 128], BF16)),
                 EE(nc.sbuf_tensor("PbfPBt", [128, 128], BF16))]
        PtB = [EE(nc.sbuf_tensor("PtA", [128, 128], BF16)),
               EE(nc.sbuf_tensor("PtBt", [128, 128], BF16))]
        selr_sb = EE(nc.sbuf_tensor("selr_sb", [16, 16, 128], BF16))
        bmt = EE(nc.sbuf_tensor("bmt", [16, T // 2], BF16))
        cmt = EE(nc.sbuf_tensor("cmt", [16, T // 2], BF16))

        pb = [EE(nc.psum_tensor(f"pb{i}", [128, 512], F32)) for i in range(7)]
        ptrb = EE(nc.psum_tensor("ptrb", [128, 1024], BF16))

        sems = {}
        for s in ("qS", "qG", "sv", "ss", "st", "sg", "sy",
                  "bsy", "bsg", "bss", "bsv", "bst"):
            sems[s] = EE(nc.semaphore(s))
        P = Prog(sems)
        xcb = arena[:, AR_N - 4096:AR_N - 2048].bitcast(BF16)\
            .rearrange("p (c b) -> p c b", b=BLK)
        xsqb = arena[:, AR_N - 2048:AR_N].bitcast(BF16)\
            .rearrange("p (c b) -> p c b", b=BLK)
        s1row = rows[:, 0, :]
        s2row = rows[:, 1, :]
        b1s = arf(8192, 4096).rearrange("p (c b) -> p c b", b=BLK)
        b2s = arf(12288, 4096).rearrange("p (c b) -> p c b", b=BLK)
        tmp8 = arf(16384, 4096).rearrange("p (c b) -> p c b", b=BLK)

        def t_mm(out, lhsT, rhs, start, stop, inc=None):
            def fn(e, out=out, lhsT=lhsT, rhs=rhs, start=start, stop=stop):
                return e.matmul(out=out, lhsT=lhsT, rhs=rhs, start=start,
                                stop=stop)
            P.emit("tensor", fn, inc=inc)

        P.dma("sync", e8_sb[:, :], E8c[:, :])
        P.dma("sync", e16_sb[:, :], E16w[:, :])
        P.dma("sync", bkw_sb[:, :], BKw[:, :])
        P.dma("gpsimd", idn_sb[:, :], IDN[:, :], inc="qG")
        P.dma("sync", oh_a[:, :], Q_OH[0:128, :])
        P.dma("sync", oh_b[:, :], Q_OH[128:225, :])
        P.dma("gpsimd", bt_af[:, :], Q_btab[0:128, :], inc="qG")
        P.dma("gpsimd", bt_bf[:, :], Q_btab[128:225, :], inc="qG")
        P.dma("sync", nwrow[0:1, :], L0_normw.unsqueeze(0))
        P.dma("sync", vecs[:, 0:16], L0_convb.rearrange("(c p) -> p c", p=128), slow=True)
        P.dma("sync", vecs[:, 16:32], L0_wd2b.rearrange("(c p) -> p c", p=128), slow=True)
        P.dma("sync", vecs[:, 32:48], L0_WD.rearrange("(c p) -> p c", p=128), slow=True)
        P.dma("sync", vecs[:, 48:56], FF_bias.rearrange("(c p) -> p c", p=128), slow=True)
        P.dma("gpsimd", Aneg[:, :, :],
              L0_Alog.rearrange("(c p) n -> p c n", p=128), inc="qG")
        P.dma("sync", qb_sc[:, :], Q_bqkv.rearrange("(c p) -> p c", p=128), slow=True)
        P.dma("gpsimd", qbv_row[0:1, :], Q_bqkv[2048:3072].unsqueeze(0), inc="qG")
        P.dma("gpsimd", selr_sb[:, :, :], SELR[:, :, :], inc="qG")
        P.dma("sync", pbias[:, :], Q_bproj.rearrange("(c p) -> p c", p=128), slow=True)
        P.dma("sync", murc[:, :], W1_mur.rearrange("(c p) -> p c", p=128), slow=True)
        P.dma("sync", mukc[:, :], W1_muk.rearrange("(c p) -> p c", p=128), slow=True)
        P.dma("gpsimd", convw_sb[:, :, :],
              L0_convw.rearrange("(c p) k -> p c k", p=128), inc="qG")
        P.dma("gpsimd", d1bc_sb[:, :, :],
              L0_d1bcT.rearrange("(c p) k -> p c k", p=128), inc="qG")
        P.dma("gpsimd", wd2_sb[:, :, :],
              L0_wd2T.rearrange("r (c p) -> r c p", p=128), inc="qG")
        init_loads = P.val("qS")
        init_loads_g = P.val("qG")

        P.wait("vector", "qS", init_loads)
        P.wait("vector", "qG", init_loads_g)
        P.emit("vector", lambda e: e.memset(csts[:, 0:1], EPS))
        P.emit("vector", lambda e: e.memset(csts[:, 1:2], 0.5 / D))
        P.emit("vector", lambda e: e.memset(csts[:, 2:3], 0.125))
        P.emit("vector", lambda e: e.memset(ones_b[:], 1.0))
        P.emit("vector", lambda e: e.memset(ones_f1[:], 1.0))
        P.emit("vector", lambda e: e.memset(onesb_1[:], 1.0))
        P.emit("vector", lambda e: e.tensor_scalar(
            out=vecs[:, 48:56], in0=vecs[:, 48:56], scalar1=0.5, scalar2=None,
            op0=ALU.mult))
        P.emit("vector", lambda e: e.tensor_scalar(
            out=qb_sc[:, 0:8], in0=qb_sc[:, 0:8], scalar1=0.125, scalar2=None,
            op0=ALU.mult), inc="sv")
        P.wait("scalar", "qS", init_loads)
        P.wait("scalar", "qG", init_loads_g)
        P.emit("scalar", lambda e: e.activation(
            out=Aneg[:, :, :], in_=Aneg[:, :, :], func=AF.Exp))
        P.emit("scalar", lambda e: e.copy(out=bt_a[:], in_=bt_af[:]))
        P.emit("scalar", lambda e: e.copy(out=bt_b[:], in_=bt_bf[:]), inc="ss")
        P.wait("vector", "ss")
        P.emit("vector", lambda e: e.tensor_scalar(
            out=Aneg[:, :, :], in0=Aneg[:, :, :], scalar1=-1.0, scalar2=None,
            op0=ALU.mult), inc="sv")
        P.wait("tensor", "ss")
        t_mm(pb[0][0:64, 0:16], oh_a[:, :], bt_a[:, :], True, False)
        t_mm(pb[0][0:64, 0:16], oh_b[:, :], bt_b[:, :], False, True, inc="st")
        P.wait("scalar", "st")
        P.emit("scalar", lambda e: e.copy(out=bias8[:, :], in_=pb[0][0:64, 0:16]),
               inc="ss")
        P.wait("sync", "ss")
        P.dma("sync", scr_sm[:, :], bias8[:, :])
        P.wait("sync", "qS")
        P.dma("sync", bcolb[:, :, :],
              scr_sm.rearrange("(a b) h -> a b h", b=8))
        P.wait("vector", "qS")
        P.emit("vector", lambda e: e.tensor_copy(
            out=bcol[:, :, :, :],
            in_=bcolb.rearrange("a b h -> a h b").unsqueeze(2)
                .broadcast_to((8, 16, 16, 8))), inc="sv")
        P.barrier()

        xbB = [xblk, arf(0, 4096).rearrange("p (c b) -> p c b", b=BLK)]

        def ln_stats(want_sl2=False, xsrc=None):
            xs = scr_x3 if xsrc is None else xsrc
            con_sv = {}
            con_ss = {}
            qvs = {}

            def lnst_dma(blk):
                if blk < 2:
                    P.wait("sync", "sv")
                    P.wait("sync", "ss")
                    P.wait("sync", "st")
                else:
                    P.wait("sync", "sv", con_sv[blk - 2])
                    P.wait("sync", "ss", con_ss[blk - 2])
                P.dma("sync", xbB[blk % 2][:, :, :],
                      xs[:, :, blk * BLK:(blk + 1) * BLK])
                qvs[blk] = P.val("qS")

            lnst_dma(0)
            lnst_dma(1)
            for blk in range(NBLK):
                sl = slice(blk * BLK, blk * BLK + BLK)
                xb = xbB[blk % 2]
                P.wait("vector", "qS", qvs[blk])
                P.wait("vector", "st")
                P.emit("vector", lambda e, xb=xb: e.tensor_copy(
                    out=xcb[:, :, :], in_=xb[:, :, :]), inc="sv")
                con_sv[blk] = P.val("sv")
                P.wait("scalar", "qS", qvs[blk])
                P.wait("scalar", "st")
                P.emit("scalar", lambda e, xb=xb: e.activation(
                    out=xsqb[:, :, :], in_=xb[:, :, :], func=AF.Square),
                    inc="ss")
                con_ss[blk] = P.val("ss")
                P.wait("tensor", "sv")
                P.wait("tensor", "ss")
                P.bank_wait(0)
                P.bank_wait(1)
                for c in range(8):
                    t_mm(pb[0][0:1, 0:BLK], ones_b[:, :], xcb[:, c, :],
                         c == 0, c == 7)
                for c in range(8):
                    t_mm(pb[1][0:1, 0:BLK], ones_b[:, :], xsqb[:, c, :],
                         c == 0, c == 7, inc="st" if c == 7 else None)
                P.wait("scalar", "st")
                P.emit("scalar", lambda e, sl=sl: e.copy(
                    out=s1row[:, sl], in_=pb[0][0:1, 0:BLK]))
                P.emit("scalar", lambda e, sl=sl: e.copy(
                    out=s2row[:, sl], in_=pb[1][0:1, 0:BLK]), inc="ss")
                P.bank_set(0, "ss")
                P.bank_set(1, "ss")
                if blk + 2 < NBLK:
                    lnst_dma(blk + 2)
            P.wait("sync", "ss")
            P.dma("sync", st16[:, 0, :], s1row[:, :])
            P.dma("sync", st16[:, 1, :], s2row[:, :])
            qv = P.val("qS")
            P.wait("vector", "qS", qv)
            sc = 1.0 / D
            P.emit("vector", lambda e: e.tensor_scalar(
                out=st16[:, 2, :], in0=st16[:, 0, :], scalar1=sc, scalar2=None,
                op0=ALU.mult), inc="sv")
            P.fence("vector")
            P.emit("vector", lambda e: e.tensor_tensor(
                out=st16[:, 6, :], in0=st16[:, 2, :], in1=st16[:, 2, :],
                op=ALU.mult), inc="sv")
            P.fence("vector")
            P.emit("vector", lambda e: e.scalar_tensor_tensor(
                out=st16[:, 3, :], in0=st16[:, 1, :], scalar=sc,
                in1=st16[:, 6, :], op0=ALU.mult, op1=ALU.subtract), inc="sv")
            P.wait("scalar", "sv")
            P.emit("scalar", lambda e: e.activation(
                out=st16[:, 6, :], in_=st16[:, 3, :], func=AF.Sqrt, bias=csts[:, 0:1]),
                inc="ss")
            P.wait("vector", "ss")
            P.emit("vector", lambda e: e.reciprocal(
                out=st16[:, 4, :], in_=st16[:, 6, :]), inc="sv")
            P.fence("vector")
            P.emit("vector", lambda e: e.tensor_tensor(
                out=st16[:, 5, :], in0=st16[:, 2, :], in1=st16[:, 4, :],
                op=ALU.mult), inc="sv")
            if want_sl2:
                P.emit("vector", lambda e: e.tensor_tensor(
                    out=st16[:, 6, :], in0=st16[:, 3, :], in1=st16[:, 4, :],
                    op=ALU.mult), inc="sv")
                P.fence("vector")
                P.emit("vector", lambda e: e.tensor_tensor(
                    out=st16[:, 6, :], in0=st16[:, 6, :], in1=st16[:, 4, :],
                    op=ALU.mult), inc="sv")
                P.wait("scalar", "sv")
                P.emit("scalar", lambda e: e.activation(
                    out=st16[:, 7, :], in_=st16[:, 6, :], func=AF.Sqrt,
                    bias=csts[:, 0:1]), inc="ss")
                P.wait("vector", "ss")
                P.emit("vector", lambda e: e.reciprocal(
                    out=st16[:, 7, :], in_=st16[:, 7, :]), inc="sv")
                P.fence("vector")
                P.emit("vector", lambda e: e.tensor_tensor(
                    out=st16[:, 8, :], in0=st16[:, 4, :], in1=st16[:, 7, :],
                    op=ALU.mult), inc="sv")
                P.fence("vector")
                P.emit("vector", lambda e: e.tensor_tensor(
                    out=st16[:, 9, :], in0=st16[:, 2, :], in1=st16[:, 8, :],
                    op=ALU.mult), inc="sv")
            P.wait("sync", "sv")
            if want_sl2:
                P.dma("sync", rows[0:1, 0, :], st16[:, 8, :])
                P.dma("sync", rows[0:1, 1, :], st16[:, 9, :])
            else:
                P.dma("sync", rows[0:1, 0, :], st16[:, 4, :])
                P.dma("sync", rows[0:1, 1, :], st16[:, 5, :])
            P.wait("vector", "qS")
            P.wait("tensor", "qS")

        def xn_build(r_idx=0, mr_idx=1, use_nw=False, sl2=False, roll=False,
                     xprime_only=False, xsrc=None):
            xs = scr_x3 if xsrc is None else xsrc
            con_sv = {}
            qvs = {}

            def xnb_dma(blk):
                if blk < 2:
                    P.wait("sync", "sv")
                    P.wait("sync", "ss")
                else:
                    P.wait("sync", "sv", con_sv[blk - 2])
                P.wait("sync", "qS")
                P.dma("sync", xbB[blk % 2][:, :, :],
                      xs[:, :, blk * BLK:(blk + 1) * BLK])
                qvs[blk] = P.val("qS")

            xnb_dma(0)
            xnb_dma(1)
            for blk in range(NBLK):
                sl = slice(blk * BLK, blk * BLK + BLK)
                xb = xbB[blk % 2]
                qv = qvs[blk]
                P.wait("tensor", "qS", qv)
                if not use_nw:
                    P.bank_wait(2)
                    P.bank_wait(3)
                    t_mm(pb[2][:, 0:BLK], ones_f1[:, :], rows[0:1, r_idx, sl],
                         True, True)
                    t_mm(pb[3][:, 0:BLK], ones_f1[:, :], rows[0:1, mr_idx, sl],
                         True, True, inc="st")
                if use_nw:
                    for c in range(8):
                        P.bank_wait(2)
                        P.bank_wait(3)
                        t_mm(pb[2][:, 0:BLK], nwrow[0:1, c * 128:(c + 1) * 128],
                             rows[0:1, r_idx, sl], True, True)
                        t_mm(pb[3][:, 0:BLK], nwrow[0:1, c * 128:(c + 1) * 128],
                             rows[0:1, mr_idx, sl], True, True, inc="st")
                        stv = P.val("st")
                        P.wait("scalar", "st", stv)
                        P.emit("scalar", lambda e, c=c: e.copy(
                            out=b1s[:, c, :], in_=pb[2][:, 0:BLK]))
                        P.emit("scalar", lambda e, c=c: e.copy(
                            out=b2s[:, c, :], in_=pb[3][:, 0:BLK]), inc="ss")
                        P.bank_set(2, "ss")
                        P.bank_set(3, "ss")
                    b1v = b1s[:, :, :]
                    b2v = b2s[:, :, :]
                    ssv = P.val("ss")
                    P.wait("vector", "ss", ssv)
                else:
                    stv = P.val("st")
                    P.wait("scalar", "st", stv)
                    P.emit("scalar", lambda e: e.copy(
                        out=b1f[:, :], in_=pb[2][:, 0:BLK]))
                    P.emit("scalar", lambda e: e.copy(
                        out=b2f[:, :], in_=pb[3][:, 0:BLK]), inc="ss")
                    P.bank_set(2, "ss")
                    P.bank_set(3, "ss")
                    ssv = P.val("ss")
                    P.wait("vector", "ss", ssv)
                    b1v = b1f[:, :].unsqueeze(1).broadcast_to((128, 8, BLK))
                    b2v = b2f[:, :].unsqueeze(1).broadcast_to((128, 8, BLK))
                P.wait("vector", "qS", qv)
                t1_sv = {}
                for hh in range(2):
                    hsl = slice(4 * hh, 4 * hh + 4)
                    P.emit("vector", lambda e, b1v=b1v, hsl=hsl, xb=xb:
                           e.tensor_tensor(
                               out=tmp8[:, hsl, :], in0=xb[:, hsl, :],
                               in1=b1v[:, hsl, :], op=ALU.mult), inc="sv")
                    t1_sv[hh] = P.val("sv")
                if xprime_only:
                    t2_sv = {}
                    for hh in range(2):
                        hsl = slice(4 * hh, 4 * hh + 4)
                        P.wait("vector", "sv", t1_sv[hh])
                        P.emit("vector", lambda e, b2v=b2v, hsl=hsl: e.tensor_tensor(
                            out=tmp8[:, hsl, :], in0=tmp8[:, hsl, :],
                            in1=b2v[:, hsl, :], op=ALU.subtract), inc="sv")
                        t2_sv[hh] = P.val("sv")
                    for hh in range(2):
                        hsl = slice(4 * hh, 4 * hh + 4)
                        P.wait("vector", "sv", t2_sv[hh])
                        P.emit("vector", lambda e, hsl=hsl, xb=xb: e.tensor_tensor(
                            out=xb[:, hsl, :], in0=xb[:, hsl, :],
                            in1=tmp8[:, hsl, :], op=ALU.add), inc="sv")
                elif not roll:
                    for hh in range(2):
                        hsl = slice(4 * hh, 4 * hh + 4)
                        P.wait("vector", "sv", t1_sv[hh])
                        P.emit("vector", lambda e, sl=sl, b2v=b2v, hsl=hsl:
                               e.tensor_tensor(
                                   out=xnbf[:, hsl, sl], in0=tmp8[:, hsl, :],
                                   in1=b2v[:, hsl, :], op=ALU.subtract),
                               inc="sv")
                elif blk == 0:
                    P.wait("vector", "sv", t1_sv[1])
                    P.emit("vector", lambda e, b2v=b2v: e.tensor_tensor(
                        out=xnbf[:, :, T - 4:T], in0=tmp8[:, :, 0:4],
                        in1=b2v[:, :, 0:4], op=ALU.subtract))
                    P.emit("vector", lambda e, b2v=b2v: e.tensor_tensor(
                        out=xnbf[:, :, 0:508], in0=tmp8[:, :, 4:512],
                        in1=b2v[:, :, 4:512], op=ALU.subtract), inc="sv")
                else:
                    P.wait("vector", "sv", t1_sv[1])
                    P.emit("vector", lambda e, blk=blk, b2v=b2v: e.tensor_tensor(
                        out=xnbf[:, :, blk * BLK - 4:blk * BLK + 508],
                        in0=tmp8[:, :, :], in1=b2v, op=ALU.subtract), inc="sv")
                P.wait("tensor", "sv")
                con_sv[blk] = P.val("sv")
                if xprime_only:
                    P.wait("sync", "sv")
                    P.dma("sync", scr_x3[:, :, sl], xb[:, :, :])
                if blk + 2 < NBLK:
                    xnb_dma(blk + 2)
            P.wait("vector", "qS")

        def ffn_precompute():
            Cb = arb(0, 4096).rearrange("p (c m) -> p c m", m=1024)
            Sb = arb(4096, 4096).rearrange("p (c m) -> p c m", m=1024)
            Wr_b = arb(8192, 4096).rearrange("p (c o) -> p c o", o=1024)
            Wi_b = arb(12288, 4096).rearrange("p (c o) -> p c o", o=1024)
            PTf = arb(16384, 4096)
            QnTf = arb(20480, 4096)
            PTb = PTf.rearrange("p (c d) -> p c d", d=1024)
            QnTb = QnTf.rearrange("p (c d) -> p c d", d=1024)
            Mhif = arb(8192, 4096)
            Mlof = arb(12288, 4096)
            stg = arf(16384, 2048)

            P.wait("sync", "sv")
            P.wait("sync", "ss")
            P.dma("sync", Cb, CC.rearrange("(c p) m -> p c m", p=128))
            P.dma("sync", Sb, SS.rearrange("(c p) m -> p c m", p=128))
            for (src, dstf) in ((FF_Wr, arb(8192, 4096)), (FF_Wi, arb(12288, 4096))):
                for h in range(4):
                    P.dma("sync", stg.rearrange("p (a n) -> p a n", n=1024),
                          src.rearrange("(c p) o -> p c o", p=128)[:, 2 * h:2 * h + 2, :])
                    qv = P.val("qS")
                    P.wait("vector", "qS", qv)
                    P.emit("vector", lambda e, dstf=dstf, h=h: e.tensor_copy(
                        out=dstf[:, h * 2048:h * 2048 + 2048], in_=stg),
                        inc="sv")
                    P.wait("sync", "sv")

            def stage_AB(dstf, w1, r1, w2, r2):
                for oc in range(8):
                    for dh in range(2):
                        bk = 2 + (2 * oc + dh) % 4
                        P.bank_wait(bk)
                        outp = pb[bk][:, :]
                        for bc in range(8):
                            t_mm(outp, w1[:, bc, oc * 128:(oc + 1) * 128],
                                 r1[:, bc, dh * 512:dh * 512 + 512],
                                 bc == 0, False)
                        for bc in range(8):
                            t_mm(outp, w2[:, bc, oc * 128:(oc + 1) * 128],
                                 r2[:, bc, dh * 512:dh * 512 + 512],
                                 False, bc == 7, inc="st" if bc == 7 else None)
                        stv = P.val("st")
                        P.wait("scalar", "st", stv)
                        off = oc * 1024 + dh * 512
                        P.emit("scalar", lambda e, off=off, outp=outp: e.copy(
                            out=dstf[:, off:off + 512], in_=outp), inc="ss")
                        P.bank_set(bk, "ss")

            P.wait("tensor", "qS")
            P.wait("tensor", "sv")
            stage_AB(PTf, Wr_b, Cb, Wi_b, Sb)
            P.wait("vector", "ss")
            P.emit("vector", lambda e: e.tensor_scalar(
                out=arb(12288, 4096), in0=arb(12288, 4096), scalar1=-1.0,
                scalar2=None, op0=ALU.mult), inc="sv")
            P.wait("tensor", "sv")
            stage_AB(QnTf, Wi_b, Cb, Wr_b, Sb)
            P.wait("tensor", "ss")

            for dc in range(8):
                for mh in range(2):
                    bk = 2 + (2 * dc + mh) % 4
                    P.bank_wait(bk)
                    outp = pb[bk][:, :]
                    for oc in range(8):
                        t_mm(outp, PTb[:, oc, dc * 128:(dc + 1) * 128],
                             Cb[:, oc, mh * 512:mh * 512 + 512], oc == 0, False)
                    for oc in range(8):
                        t_mm(outp, QnTb[:, oc, dc * 128:(dc + 1) * 128],
                             Sb[:, oc, mh * 512:mh * 512 + 512], False, oc == 7,
                             inc="st" if oc == 7 else None)
                    stv = P.val("st")
                    P.wait("scalar", "st", stv)
                    off = dc * 1024 + mh * 512
                    P.emit("scalar", lambda e, off=off, outp=outp: e.activation(
                        out=Mhif[:, off:off + 512], in_=outp, func=AF.Copy,
                        scale=csts[:, 1:2]), inc="ss")
                    ssv = P.val("ss")
                    P.wait("vector", "ss", ssv)
                    P.emit("vector", lambda e, off=off, outp=outp:
                           e.scalar_tensor_tensor(
                               out=Mlof[:, off:off + 512], in0=outp,
                               scalar=0.5 / D, in1=Mhif[:, off:off + 512],
                               op0=ALU.mult, op1=ALU.subtract), inc="sv")
                    P.bank_set(bk, "sv")
            P.wait("sync", "sv")
            P.dma("sync", scr_M[:, 0:8192], arb(8192, 4096))
            P.dma("sync", scr_M[:, 8192:16384], arb(12288, 4096))
            P.wait("sync", "qS")

        def ffn_apply(xsrc=None):
            xs = scr_x3 if xsrc is None else xsrc
            Mhi = arb(0, 4096).rearrange("p (c m) -> p c m", m=1024)
            Mlo = arb(4096, 4096).rearrange("p (c m) -> p c m", m=1024)
            xfull = arf(8192, 16384).rearrange("p (c t) -> p c t", t=T)
            P.wait("sync", "sv")
            P.wait("sync", "ss")
            P.wait("sync", "st")
            P.dma("sync", arb(0, 4096), scr_M[:, 0:8192])
            P.dma("sync", arb(4096, 4096), scr_M[:, 8192:16384])
            for blk in range(NBLK):
                P.dma("sync", xfull[:, :, blk * BLK:(blk + 1) * BLK],
                      xs[:, :, blk * BLK:(blk + 1) * BLK])
            qv = P.val("qS")
            P.wait("tensor", "qS", qv)
            for mc in range(8):
                for blk in range(NBLK):
                    P.bank_wait(2 + blk)
                first = True
                for dc in range(8):
                    for Mv in (Mhi, Mlo):
                        for blk in range(NBLK):
                            t_mm(pb[2 + blk][:, :],
                                 Mv[:, dc, mc * 128:(mc + 1) * 128],
                                 xnbf[:, dc, blk * BLK:(blk + 1) * BLK],
                                 first, (dc == 7 and Mv is Mlo),
                                 inc="st" if (dc == 7 and Mv is Mlo and blk == 3)
                                 else None)
                        first = False
                stv = P.val("st")
                P.wait("vector", "st", stv)
                for blk in range(NBLK):
                    P.emit("vector", lambda e, mc=mc, blk=blk:
                           e.scalar_tensor_tensor(
                               out=xfull[:, mc, blk * BLK:(blk + 1) * BLK],
                               in0=pb[2 + blk][:, :],
                               scalar=vecs[:, 48 + mc:49 + mc],
                               in1=xfull[:, mc, blk * BLK:(blk + 1) * BLK],
                               op0=ALU.add, op1=ALU.add), inc="sv")
                    P.bank_set(2 + blk, "sv")
            P.wait("sync", "sv")
            for blk in range(NBLK):
                P.dma("sync", scr_x3[:, :, blk * BLK:(blk + 1) * BLK],
                      xfull[:, :, blk * BLK:(blk + 1) * BLK])
            P.wait("sync", "qS")

        def rwkv_apply(out_dst=None):
            xdst = scr_x3 if out_dst is None else out_dst
            Wr_v = arb(0, 4096).rearrange("p (c n) -> p c n", n=1024)
            Wk_v = arb(4096, 4096).rearrange("p (c n) -> p c n", n=1024)
            Wv_v = arb(8192, 4096).rearrange("p (c n) -> p c n", n=1024)
            xnb = arf(12288, 4104).rearrange("p (c n) -> p c n", n=513)
            xrb = arb(16392, 2048).rearrange("p (c n) -> p c n", n=BLK)
            xkb = arb(18440, 2048).rearrange("p (c n) -> p c n", n=BLK)
            rblk = arb(20488, 2048).rearrange("p (c n) -> p c n", n=BLK)
            kab = xrb
            krelu = tmp2
            vblk = xkb
            P.wait("gpsimd", "sv")
            P.wait("gpsimd", "ss")
            P.wait("gpsimd", "st")
            P.dma("gpsimd", Wr_v, W1_WrT.rearrange("(c p) n -> p c n", p=128),
                  inc="qG")
            P.dma("gpsimd", Wk_v, W1_WkT.rearrange("(c p) n -> p c n", p=128),
                  inc="qG")
            P.dma("gpsimd", Wv_v, W1_WvT.rearrange("(c p) n -> p c n", p=128),
                  inc="qG")
            ldg = P.val("qG")
            P.emit("vector", lambda e: e.memset(xnb[:, :, 0:1], 0.0), inc="sv")
            for blk in range(NBLK):
                sl = slice(blk * BLK, blk * BLK + BLK)
                P.wait("sync", "sv")
                P.wait("sync", "ss")
                P.dma("sync", xblk[:, :, :], scr_x3[:, :, sl])
                qv = P.val("qS")
                P.wait("tensor", "qS", qv)
                P.bank_wait(2)
                P.bank_wait(3)
                t_mm(pb[2][:, 0:BLK], ones_f1[:, :], rows[0:1, 0, sl],
                     True, True)
                t_mm(pb[3][:, 0:BLK], ones_f1[:, :], rows[0:1, 1, sl],
                     True, True, inc="st")
                stv = P.val("st")
                P.wait("scalar", "st", stv)
                P.emit("scalar", lambda e: e.copy(out=b1f[:, :], in_=pb[2][:, 0:BLK]))
                P.emit("scalar", lambda e: e.copy(out=b2f[:, :], in_=pb[3][:, 0:BLK]),
                       inc="ss")
                P.bank_set(2, "ss")
                P.bank_set(3, "ss")
                ssv = P.val("ss")
                P.wait("vector", "ss", ssv)
                P.wait("vector", "qS", qv)
                b1v2 = b1f[:, :].unsqueeze(1).broadcast_to((128, 2, BLK))
                b2v2 = b2f[:, :].unsqueeze(1).broadcast_to((128, 2, BLK))
                for g in range(4):
                    gsl = slice(2 * g, 2 * g + 2)
                    P.emit("vector", lambda e, gsl=gsl, b1v2=b1v2: e.tensor_tensor(
                        out=tmp2[:, :, :], in0=xblk[:, gsl, :], in1=b1v2,
                        op=ALU.mult), inc="sv")
                    P.fence("vector")
                    P.emit("vector", lambda e, gsl=gsl, b2v2=b2v2: e.tensor_tensor(
                        out=xnb[:, gsl, 1:513], in0=tmp2[:, :, :], in1=b2v2,
                        op=ALU.subtract), inc="sv")
                    P.fence("vector")
                for g in range(4):
                    gsl = slice(2 * g, 2 * g + 2)
                    P.emit("vector", lambda e, gsl=gsl: e.tensor_tensor(
                        out=tmp2[:, :, :], in0=xnb[:, gsl, 1:513],
                        in1=xnb[:, gsl, 0:512], op=ALU.subtract), inc="sv")
                    P.fence("vector")
                    for ci in range(2):
                        c = 2 * g + ci
                        P.emit("vector", lambda e, c=c, ci=ci: e.scalar_tensor_tensor(
                            out=xrb[:, c, :], in0=tmp2[:, ci, :],
                            scalar=murc[:, c:c + 1], in1=xnb[:, c, 0:512],
                            op0=ALU.mult, op1=ALU.add), inc="sv")
                        P.emit("vector", lambda e, c=c, ci=ci: e.scalar_tensor_tensor(
                            out=xkb[:, c, :], in0=tmp2[:, ci, :],
                            scalar=mukc[:, c:c + 1], in1=xnb[:, c, 0:512],
                            op0=ALU.mult, op1=ALU.add), inc="sv")
                    P.fence("vector")
                P.emit("vector", lambda e: e.tensor_copy(
                    out=xnb[:, :, 0:1], in_=xnb[:, :, 512:513]), inc="sv")
                P.fence("vector")
                svv = P.val("sv")
                P.wait("tensor", "sv", svv)
                P.wait("tensor", "qG", ldg)
                for mc in range(8):
                    bk = 2 + (mc % 4)
                    P.bank_wait(bk)
                    outp = pb[bk][:, :]
                    for kc in range(8):
                        t_mm(outp, Wr_v[:, kc, mc * 128:(mc + 1) * 128],
                             xrb[:, kc, :], kc == 0, kc == 7,
                             inc="st" if kc == 7 else None)
                    stv = P.val("st")
                    P.wait("scalar", "st", stv)
                    P.emit("scalar", lambda e, mc=mc, outp=outp: e.activation(
                        out=rblk[:, mc, :], in_=outp, func=AF.Sigmoid),
                        inc="ss")
                    P.bank_set(bk, "ss")
                for mc in range(8):
                    bk = 2 + (mc % 4)
                    P.bank_wait(bk)
                    outp = pb[bk][:, :]
                    for kc in range(8):
                        t_mm(outp, Wk_v[:, kc, mc * 128:(mc + 1) * 128],
                             xkb[:, kc, :], kc == 0, kc == 7,
                             inc="st" if kc == 7 else None)
                    stv = P.val("st")
                    P.wait("scalar", "st", stv)
                    P.emit("scalar", lambda e, mc=mc, outp=outp: e.activation(
                        out=krelu[:, mc % 2, :], in_=outp, func=AF.Relu),
                        inc="ss")
                    P.fence("scalar")
                    P.emit("scalar", lambda e, mc=mc: e.activation(
                        out=kab[:, mc, :], in_=krelu[:, mc % 2, :],
                        func=AF.Square), inc="ss")
                    P.bank_set(bk, "ss")
                P.wait("tensor", "ss")
                for mc in range(8):
                    bk = 2 + (mc % 4)
                    P.bank_wait(bk)
                    outp = pb[bk][:, :]
                    for kc in range(8):
                        t_mm(outp, Wv_v[:, kc, mc * 128:(mc + 1) * 128],
                             kab[:, kc, :], kc == 0, kc == 7,
                             inc="st" if kc == 7 else None)
                    stv = P.val("st")
                    P.wait("vector", "st", stv)
                    P.emit("vector", lambda e, mc=mc, outp=outp: e.tensor_tensor(
                        out=tmp2[:, mc % 2, :], in0=rblk[:, mc, :], in1=outp,
                        op=ALU.mult), inc="sv")
                    P.fence("vector")
                    P.emit("vector", lambda e, mc=mc: e.scalar_tensor_tensor(
                        out=xblk[:, mc, :], in0=tmp2[:, mc % 2, :], scalar=0.5,
                        in1=xblk[:, mc, :], op0=ALU.mult, op1=ALU.add),
                        inc="sv")
                    P.bank_set(bk, "sv")
                P.wait("sync", "sv")
                P.dma("sync", xdst[:, :, sl], xblk[:, :, :])
                P.wait("tensor", "sv")

        def swin_apply():
            wqkv = arb(0, 12288).rearrange("p (c n) -> p c n", n=3072)
            wproj = arb(12288, 4096).rearrange("p (c n) -> p c n", n=1024)
            qtile = arb(16384, 1024).rearrange("p (c n) -> p c n", n=256)
            ktile = arb(17408, 1024).rearrange("p (c n) -> p c n", n=256)
            vtile = arb(18432, 1024).rearrange("p (c n) -> p c n", n=1024)
            attno = arb(19456, 1024).rearrange("p (c n) -> p c n", n=256)
            pout = arf(20480, 4096).rearrange("p (c n) -> p c n", n=BLK)
            P.wait("gpsimd", "sv")
            P.wait("gpsimd", "ss")
            P.wait("gpsimd", "st")
            P.dma("gpsimd", wqkv, Q_wqkv.rearrange("(c p) n -> p c n", p=128),
                  inc="qG")
            P.dma("gpsimd", wproj, Q_wproj.rearrange("(c p) n -> p c n", p=128),
                  inc="qG")
            ldg = P.val("qG")
            P.wait("tensor", "qG", ldg)
            for pair in range(8):
                ts2 = slice(pair * 256, pair * 256 + 256)
                blk = pair // 2
                for part, wof, dst, scl_, bof in (
                        (0, 0, qtile, csts[:, 2:3], 0),
                        (1, 1024, ktile, None, 8)):
                    for oc in range(8):
                        bk = 4 + (oc % 2)
                        P.bank_wait(bk)
                        outp = pb[bk][:, 0:256]
                        for kc in range(8):
                            t_mm(outp, wqkv[:, kc, wof + oc * 128:wof + (oc + 1) * 128],
                                 xnbf[:, kc, ts2], kc == 0, kc == 7,
                                 inc="st" if kc == 7 else None)
                        stv = P.val("st")
                        P.wait("scalar", "st", stv)
                        if scl_ is None:
                            P.emit("scalar", lambda e, oc=oc, dst=dst, bof=bof,
                                   outp=outp: e.activation(
                                       out=dst[:, oc, :], in_=outp,
                                       func=AF.Identity,
                                       bias=qb_sc[:, bof + oc:bof + oc + 1]),
                                   inc="ss")
                        else:
                            P.emit("scalar", lambda e, oc=oc, dst=dst, bof=bof,
                                   outp=outp, scl_=scl_: e.activation(
                                       out=dst[:, oc, :], in_=outp,
                                       func=AF.Identity, scale=scl_,
                                       bias=qb_sc[:, bof + oc:bof + oc + 1]),
                                   inc="ss")
                        P.bank_set(bk, "ss")
                for ti in range(2):
                    ts = slice(pair * 256 + ti * 128, pair * 256 + ti * 128 + 128)
                    for vb in range(2):
                        bk = 4 + (2 * ti + vb) % 3
                        P.bank_wait(bk)
                        outp = pb[bk][:, :]
                        for kc in range(8):
                            t_mm(outp, xnbf[:, kc, ts],
                                 wqkv[:, kc, 2048 + vb * 512:2048 + (vb + 1) * 512],
                                 kc == 0, False)
                        t_mm(outp, onesb_1[:, :],
                             qbv_row[0:1, vb * 512:(vb + 1) * 512],
                             False, True, inc="st")
                        stv = P.val("st")
                        P.wait("scalar", "st", stv)
                        P.emit("scalar", lambda e, ti=ti, vb=vb, outp=outp: e.copy(
                            out=vtile[:, ti, vb * 512:(vb + 1) * 512],
                            in_=outp), inc="ss")
                        P.bank_set(bk, "ss")
                P.wait("tensor", "ss")
                for ti in range(2):
                    P.wait("tensor", "ss")
                    P.wait("vector", "st")
                    P.wait("scalar", "st")
                    sc_st = {}
                    mx_sv = {}
                    ex_ss = {}
                    sr_sv = {}
                    ts_sv = {}
                    tr_st = {}
                    pt_ss = {}
                    pv_st = {}

                    def st_scores(h, ti=ti):
                        bk = 4 + (h % 3)
                        P.bank_wait(bk)
                        hc, hp = h // 2, (h % 2) * 64
                        qs = qtile[hp:hp + 64, hc, ti * 128:ti * 128 + 128]
                        ks = ktile[hp:hp + 64, hc, ti * 128:ti * 128 + 128]
                        t_mm(pb[bk][:, 0:128], qs, ks, True, False)
                        t_mm(pb[bk][:, 0:128], e8_sb[:, :],
                             bcol[:, h, :, :].rearrange("a w b -> a (w b)"),
                             False, False)
                        t_mm(pb[bk][:, 0:128], e16_sb[:, :], bkw_sb[:, :],
                             False, True, inc="st")
                        sc_st[h] = P.val("st")

                    def sv_max(h):
                        bk = 4 + (h % 3)
                        P.wait("vector", "st", sc_st[h])
                        P.wait("vector", "ss", ex_ss.get(h - 2, 0))
                        P.emit("vector", lambda e, h=h, bk=bk: e.reduce_max(
                            out=vecs[:, 56 + h % 2:57 + h % 2],
                            in_=pb[bk][:, 0:128],
                            axis=mybir.AxisListType.X, negate=True), inc="sv")
                        mx_sv[h] = P.val("sv")

                    def ss_exp(h):
                        bk = 4 + (h % 3)
                        P.wait("scalar", "sv", mx_sv[h])
                        P.wait("scalar", "sv", ts_sv.get(h - 2, 0))
                        P.emit("scalar", lambda e, h=h, bk=bk: e.activation(
                            out=PbfB[h % 2][:, :], in_=pb[bk][:, 0:128],
                            func=AF.Exp,
                            bias=vecs[:, 56 + h % 2:57 + h % 2]), inc="ss")
                        ex_ss[h] = P.val("ss")
                        P.bank_set(bk, "ss")

                    def sv_sum(h):
                        P.wait("vector", "ss", ex_ss[h])
                        P.wait("vector", "sv", ts_sv.get(h - 2, 0))
                        P.emit("vector", lambda e, h=h: e.reduce_sum(
                            out=vecs[:, 58 + h % 2:59 + h % 2],
                            in_=PbfB[h % 2][:, :],
                            axis=mybir.AxisListType.X), inc="sv")
                        sr_sv[h] = P.val("sv")

                    def sv_recip(h):
                        P.wait("vector", "sv", sr_sv[h])
                        P.emit("vector", lambda e, h=h: e.reciprocal(
                            out=vecs[:, 58 + h % 2:59 + h % 2],
                            in_=vecs[:, 58 + h % 2:59 + h % 2]), inc="sv")
                        sr_sv[h] = P.val("sv")

                    def sv_scale(h):
                        P.wait("vector", "sv", sr_sv[h])
                        P.wait("vector", "st", tr_st.get(h - 2, 0))
                        P.emit("vector", lambda e, h=h: e.tensor_scalar(
                            out=PbfPB[h % 2][:, :], in0=PbfB[h % 2][:, :],
                            scalar1=vecs[:, 58 + h % 2:59 + h % 2], scalar2=None,
                            op0=ALU.mult), inc="sv")
                        ts_sv[h] = P.val("sv")

                    def st_transpose(h):
                        P.wait("tensor", "sv", ts_sv[h])
                        if h >= 2:
                            P.wait("tensor", "ss", pt_ss.get(h - 2, 0))
                        P.emit("tensor", lambda e, h=h: e.transpose(
                            out=ptrb[:, (h % 2) * 128:(h % 2) * 128 + 128],
                            in_=PbfPB[h % 2][:, :], identity=idn_sb[:, :]),
                            inc="st")
                        tr_st[h] = P.val("st")

                    def ss_ptevac(h):
                        P.wait("scalar", "st", tr_st[h])
                        P.wait("scalar", "st", pv_st.get(h - 2, 0))
                        P.emit("scalar", lambda e, h=h: e.copy(
                            out=PtB[h % 2][:, :],
                            in_=ptrb[:, (h % 2) * 128:(h % 2) * 128 + 128]),
                            inc="ss")
                        pt_ss[h] = P.val("ss")

                    def st_pv(h, ti=ti):
                        P.wait("tensor", "ss", pt_ss[h])
                        if h % 2 == 0:
                            P.bank_wait(2)
                        hp = (h % 2) * 64
                        t_mm(pb[2][hp:hp + 64, 0:128],
                             vtile[:, ti, h * 64:(h + 1) * 64],
                             PtB[h % 2][:, :], True, True, inc="st")
                        pv_st[h] = P.val("st")
                        if h % 2 == 1:
                            P.wait("scalar", "st", pv_st[h])
                            P.emit("scalar", lambda e, h=h, ti=ti: e.copy(
                                out=attno[:, h // 2, ti * 128:ti * 128 + 128],
                                in_=pb[2][:, 0:128]), inc="ss")
                            P.bank_set(2, "ss")

                    for h in range(16 + 4):
                        if 0 <= h - 4 < 16:
                            st_pv(h - 4)
                        if 0 <= h - 3 < 16:
                            sv_recip(h - 3)
                            sv_scale(h - 3)
                            st_transpose(h - 3)
                            ss_ptevac(h - 3)
                        if 0 <= h - 2 < 16:
                            sv_sum(h - 2)
                        if h < 16:
                            st_scores(h)
                        if 0 <= h - 1 < 16:
                            sv_max(h - 1)
                            ss_exp(h - 1)
                P.wait("tensor", "ss")
                for oc in range(8):
                    bk = 4 + (oc % 2)
                    P.bank_wait(bk)
                    outp = pb[bk][:, 0:256]
                    for kc in range(8):
                        t_mm(outp, wproj[:, kc, oc * 128:(oc + 1) * 128],
                             attno[:, kc, :], kc == 0, kc == 7,
                             inc="st" if kc == 7 else None)
                    stv = P.val("st")
                    P.wait("scalar", "st", stv)
                    P.emit("scalar", lambda e, oc=oc, pair=pair, outp=outp: e.copy(
                        out=pout[:, oc, (pair % 2) * 256:(pair % 2) * 256 + 256],
                        in_=outp), inc="ss")
                    P.bank_set(bk, "ss")
                if pair % 2 == 1:
                    P.wait("sync", "ss")
                    P.dma("sync", scr_po3[:, :, blk * BLK:blk * BLK + BLK],
                          pout[:, :, :])
                    P.wait("scalar", "qS")
            pfB = [pout, arf(4096, 4096).rearrange("p (c n) -> p c n", n=BLK)]
            ep_sv = {}
            eqv = {}

            def ep_dma(blk):
                if blk < 2:
                    P.wait("sync", "sv")
                    P.wait("sync", "ss")
                    P.wait("sync", "st")
                else:
                    P.wait("sync", "sv", ep_sv[blk - 2])
                P.dma("sync", xbB[blk % 2][:, :, :],
                      scr_x3[:, :, blk * BLK:(blk + 1) * BLK])
                pf = pfB[blk % 2]
                if blk == 0:
                    P.dma("sync", pf[:, :, 0:4], scr_po3[:, :, T - 4:T])
                    P.dma("sync", pf[:, :, 4:512], scr_po3[:, :, 0:508])
                else:
                    P.dma("sync", pf[:, :, :],
                          scr_po3[:, :, blk * BLK - 4:blk * BLK + 508])
                eqv[blk] = P.val("qS")

            ep_dma(0)
            ep_dma(1)
            for blk in range(NBLK):
                sl = slice(blk * BLK, blk * BLK + BLK)
                xb = xbB[blk % 2]
                pf = pfB[blk % 2]
                P.wait("vector", "qS", eqv[blk])
                for c in range(8):
                    P.emit("vector", lambda e, c=c, xb=xb, pf=pf:
                           e.scalar_tensor_tensor(
                               out=xb[:, c, :], in0=pf[:, c, :],
                               scalar=pbias[:, c:c + 1], in1=xb[:, c, :],
                               op0=ALU.add, op1=ALU.add), inc="sv")
                ep_sv[blk] = P.val("sv")
                P.wait("sync", "sv", ep_sv[blk])
                P.dma("sync", scr_x3[:, :, sl], xb[:, :, :])
                if blk + 2 < NBLK:
                    ep_dma(blk + 2)


        def mamba_apply():
            inw = arb(0, 8192).rearrange("p (c n) -> p c n", n=2048)
            convinB = [arf(8192, 2052), arf(10244, 2052)]
            convaccB = [arf(12296, 2048), arf(14344, 2048)]
            x1tB = [arb(16392, 1024), arb(17416, 1024)]
            dbc = arb(23552, 1024)
            scr_x13 = scr_x1.rearrange("(c p) t -> p c t", p=128)
            scr_yg3 = scr_yg.rearrange("(c p) t -> p c t", p=128)
            HCV = T // 2

            P.wait("vector", "ss")
            P.emit("vector", lambda e: e.memset(hcar[:, :, :], 0.0))
            P.emit("vector", lambda e: e.memset(convinB[0][:, 0:3], 0.0))
            P.emit("vector", lambda e: e.memset(convinB[1][:, 0:3], 0.0),
                   inc="sv")
            P.wait("gpsimd", "sv")
            P.wait("gpsimd", "ss")
            P.wait("gpsimd", "st")
            P.dma("gpsimd", inw, L0_inwT.rearrange("(c p) n -> p c n", p=128),
                  inc="qG")
            ldg = P.val("qG")
            P.wait("tensor", "qG", ldg)
            P.wait("tensor", "sv")
            st_in = {}
            cv_end = {}
            si_ss = {}
            d1_st = {}
            sp_q = {}

            def m1_inproj(ET):
                cur = ET % 2
                banks = ((4, 5), (6, 4))
                first_evac = True
                for pr in range(2):
                    for ti in range(2):
                        bk = banks[pr][ti]
                        tb = 2 * pr + ti
                        P.bank_wait(bk)
                        for kc in range(8):
                            t_mm(pb[bk][:, :],
                                 inw[:, kc, ET * 128:(ET + 1) * 128],
                                 xnbf[:, kc, tb * BLK:(tb + 1) * BLK],
                                 kc == 0, kc == 7,
                                 inc="st" if kc == 7 else None)
                        stv = P.val("st")
                        P.wait("scalar", "st", stv)
                        if first_evac:
                            P.wait("scalar", "sv", cv_end.get(ET - 2, 0))
                            first_evac = False
                        P.emit("scalar", lambda e, tb=tb, bk=bk, cur=cur: e.copy(
                            out=convinB[cur][:, 3 + tb * BLK:3 + (tb + 1) * BLK],
                            in_=pb[bk][:, :]), inc="ss")
                        P.bank_set(bk, "ss")
                st_in[ET] = P.val("ss")

            def m1_conv(ET):
                cur = ET % 2
                convin = convinB[cur]
                convacc = convaccB[cur]
                P.wait("vector", "ss", st_in[ET])
                cv_sv = {}
                for hh in range(2):
                    P.emit("vector", lambda e, ET=ET, hh=hh, convin=convin,
                           convacc=convacc: e.tensor_scalar(
                        out=convacc[:, hh * HCV:(hh + 1) * HCV],
                        in0=convin[:, hh * HCV:hh * HCV + HCV],
                        scalar1=convw_sb[:, ET, 0:1], scalar2=vecs[:, ET:ET + 1],
                        op0=ALU.mult, op1=ALU.add), inc="sv")
                    cv_sv[(0, hh)] = P.val("sv")
                for k in range(1, 4):
                    for hh in range(2):
                        P.wait("vector", "sv", cv_sv[(k - 1, hh)])
                        P.emit("vector", lambda e, ET=ET, k=k, hh=hh,
                               convin=convin, convacc=convacc:
                               e.scalar_tensor_tensor(
                                   out=convacc[:, hh * HCV:(hh + 1) * HCV],
                                   in0=convin[:, k + hh * HCV:k + hh * HCV + HCV],
                                   scalar=convw_sb[:, ET, k:k + 1],
                                   in1=convacc[:, hh * HCV:(hh + 1) * HCV],
                                   op0=ALU.mult, op1=ALU.add), inc="sv")
                        cv_sv[(k, hh)] = P.val("sv")
                P.emit("vector", lambda e: e.engine_nop(), inc="sv")
                cv_end[ET] = P.val("sv")

            def m1_silu(ET):
                cur = ET % 2
                P.wait("scalar", "sv", cv_end[ET])
                P.wait("scalar", "st", d1_st.get(ET - 2, 0))
                P.wait("scalar", "qS", sp_q.get(ET - 2, 0))
                P.emit("scalar", lambda e, cur=cur: e.activation(
                    out=x1tB[cur][:, :], in_=convaccB[cur][:, :], func=AF.Silu),
                    inc="ss")
                si_ss[ET] = P.val("ss")

            def m1_spill(ET):
                P.wait("sync", "ss", si_ss[ET])
                P.dma("sync", scr_x13[:, :, :][0:128, ET, :], x1tB[ET % 2][:, :])
                sp_q[ET] = P.val("qS")

            def m1_d1bc(ET):
                P.wait("tensor", "ss", si_ss[ET])
                for tb in range(NBLK):
                    t_mm(pb[tb][0:96, :], d1bc_sb[:, ET, :],
                         x1tB[ET % 2][:, tb * BLK:(tb + 1) * BLK],
                         ET == 0, ET == 15,
                         inc="st" if tb == 3 else None)
                d1_st[ET] = P.val("st")

            for ET in range(16):
                m1_inproj(ET)
                if ET >= 1:
                    m1_conv(ET - 1)
                    m1_silu(ET - 1)
                    m1_spill(ET - 1)
                    m1_d1bc(ET - 1)
            m1_conv(15)
            m1_silu(15)
            m1_spill(15)
            m1_d1bc(15)
            P.wait("scalar", "st", d1_st[15])
            for tb in range(NBLK):
                P.emit("scalar", lambda e, tb=tb: e.copy(
                    out=dbc[0:96, tb * BLK:(tb + 1) * BLK], in_=pb[tb][0:96, :]),
                    inc="ss")
            P.wait("sync", "ss")
            P.barrier()

            Brep = arb(0, 8192)
            Crep = arb(8192, 8192)
            delta = arf(16384, 1024)
            AbarB = [arf(17408, 1024), arf(18432, 1024)]
            dxb = arb(19456, 512)
            bbufB = [arb(19968, 512), arb(20480, 512)]
            hbufB = [arb(20992, 512), arb(23040, 512)]
            zbufB = [arb(21504, 512), arb(22016, 512)]
            x1t2 = arb(22528, 512)
            HT = 1024
            xfl = xblk.rearrange("p c b -> p (c b)")
            deltaB2 = [delta, xfl[:, 0:1024]]
            x1t2B = [x1t2, xfl[:, 1024:1536].bitcast(BF16)]
            dxbB = [dxb, xfl[:, 1536:2048].bitcast(BF16)]
            yraw = xfl[:, 2048:2560].bitcast(BF16)
            pl = {}
            for half in range(2):
                hs = slice(half * HT, half * HT + HT)
                P.wait("sync", "st")
                P.dma("sync", bmt[:, :], dbc[64:80, hs])
                P.dma("sync", cmt[:, :], dbc[80:96, hs])
                bcq = P.val("qS")
                P.wait("tensor", "qS", bcq)
                for n in range(16):
                    for si, (src, dst) in enumerate(((bmt, Brep), (cmt, Crep))):
                        for b2 in range(2):
                            bk = 2 + b2
                            P.bank_wait(bk)
                            bank = pb[bk]
                            t_mm(bank[:, :], selr_sb[:, n, :],
                                 src[:, b2 * 512:(b2 + 1) * 512],
                                 True, True, inc="st")
                            stv = P.val("st")
                            P.wait("scalar", "st", stv)
                            P.emit("scalar", lambda e, n=n, dst=dst, b2=b2,
                                   bank=bank: e.copy(
                                       out=dst[:, n * 1024 + b2 * 512:
                                               n * 1024 + (b2 + 1) * 512],
                                       in_=bank[:, :]), inc="ss")
                            P.bank_set(bk, "ss")
                def prep_D(ET1, half=half, hs=hs):
                    P.wait("sync", "sv")
                    P.dma("sync", x1t2B[ET1 % 2][:, :], scr_x13[0:128, ET1, hs])
                    return P.val("qS")

                def prep_M(ET1, half=half):
                    P.bank_wait(4)
                    P.bank_wait(5)
                    for b2 in range(2):
                        t_mm(pb[4 + b2][:, :], wd2_sb[:, ET1, :],
                             dbc[0:64, half * HT + b2 * 512:
                                 half * HT + (b2 + 1) * 512],
                             True, True, inc="st" if b2 == 1 else None)
                    return P.val("st")

                def prep_E(ET1, stv, ss_guard, sv_guard):
                    j = ET1 % 2
                    P.wait("scalar", "st", stv)
                    P.wait("scalar", "ss", ss_guard)
                    P.wait("scalar", "sv", sv_guard)
                    for b2 in range(2):
                        P.emit("scalar", lambda e, ET1=ET1, b2=b2, j=j:
                               e.activation(
                                   out=deltaB2[j][:, b2 * 512:(b2 + 1) * 512],
                                   in_=pb[4 + b2][:, :], func=AF.Exp,
                                   bias=vecs[:, 16 + ET1:17 + ET1]), inc="ss")
                    P.bank_set(4, "ss")
                    P.bank_set(5, "ss")
                    P.fence("scalar")
                    P.emit("scalar", lambda e, j=j: e.activation(
                        out=deltaB2[j][:, :], in_=deltaB2[j][:, :],
                        func=AF.Ln, bias=1.0), inc="ss")
                    return P.val("ss")

                def prep_X(ET1, qv, ssv, war_sv):
                    j = ET1 % 2
                    P.wait("vector", "ss", ssv)
                    P.wait("vector", "qS", qv)
                    P.wait("vector", "sv", war_sv)
                    P.emit("vector", lambda e, j=j: e.tensor_tensor(
                        out=dxbB[j][:, :], in0=deltaB2[j][:, :],
                        in1=x1t2B[j][:, :], op=ALU.mult), inc="sv")
                    return P.val("sv")

                pl["exp_end"] = P.val("ss")
                pl["b15"] = P.val("sv")
                for ET in range(16):
                    cur = ET % 2
                    if ET == 0:
                        qv = prep_D(0)
                        stv0 = prep_M(0)
                        ssv = prep_E(0, stv0, P.val("ss"), P.val("sv"))
                        dxv = prep_X(0, qv, ssv, P.val("sv"))
                        pl["dxv_prev"] = dxv
                    else:
                        qv, ssv, dxv = pl["qv"], pl["ssv"], pl["dxv"]
                        pl["dxv_prev"] = dxv
                    sc14 = pl.get("scan14", dxv)
                    sc15 = pl.get("scan15", dxv)
                    scan_sv = {}
                    b_sv = {}
                    z_sv = {}
                    ymm_st = {}

                    def emit_exp(n, ET=ET, cur=cur, ssv=ssv, sc14=sc14,
                                 sc15=sc15, dxv=dxv):
                        if n >= 2:
                            P.wait("scalar", "sv", scan_sv[n - 2])
                        else:
                            P.wait("scalar", "sv", max(sc14 if n == 0 else sc15,
                                                       dxv))
                            P.wait("scalar", "ss", ssv)
                        P.emit("scalar", lambda e, ET=ET, n=n: e.activation(
                            out=AbarB[n % 2][:, :], in_=deltaB2[cur][:, :],
                            func=AF.Exp, scale=Aneg[:, ET, n:n + 1]), inc="ss")
                        return P.val("ss")

                    def emit_b(n, cur=cur, sc14=sc14, sc15=sc15, dxv=dxv):
                        if n >= 2:
                            P.wait("vector", "sv", scan_sv[n - 2])
                        else:
                            P.wait("vector", "sv", max(sc14 if n == 0 else sc15,
                                                       dxv))
                        P.emit("vector", lambda e, n=n, cur=cur: e.tensor_tensor(
                            out=bbufB[n % 2][:, :], in0=dxbB[cur][:, :],
                            in1=Brep[:, n * 1024:(n + 1) * 1024], op=ALU.mult),
                            inc="sv")
                        b_sv[n] = P.val("sv")

                    def emit_scan(n, exp_ss, ET=ET, half=half):
                        P.wait("vector", "ss", exp_ss)
                        P.wait("vector", "sv", b_sv[n])
                        if n >= 2:
                            P.wait("vector", "st", ymm_st[n - 2])
                        P.emit("vector", lambda e, ET=ET, n=n: e.tensor_tensor_scan(
                            out=hbufB[n % 2][:, :], data0=AbarB[n % 2][:, :],
                            data1=bbufB[n % 2][:, :],
                            initial=hcar[:, ET, n:n + 1],
                            op0=ALU.mult, op1=ALU.add), inc="sv")
                        scan_sv[n] = P.val("sv")

                    def emit_z(n, ET=ET, half=half):
                        P.wait("vector", "sv", scan_sv[n])
                        P.emit("vector", lambda e, n=n: e.tensor_tensor(
                            out=zbufB[n % 2][:, :], in0=hbufB[n % 2][:, :],
                            in1=Crep[:, n * 1024:(n + 1) * 1024], op=ALU.mult),
                            inc="sv")
                        z_sv[n] = P.val("sv")
                        if half == 0:
                            P.emit("vector", lambda e, ET=ET, n=n: e.tensor_copy(
                                out=hcar[:, ET, n:n + 1],
                                in_=hbufB[n % 2][:, HT - 1:HT]))
                        P.wait("tensor", "sv", z_sv[n])
                        for b2 in range(2):
                            t_mm(pb[b2][:, :], idn_sb[:, :],
                                 zbufB[n % 2][:, b2 * 512:(b2 + 1) * 512],
                                 n == 0, n == 15,
                                 inc="st" if b2 == 1 else None)
                        ymm_st[n] = P.val("st")

                    exp_ss = {n: None for n in range(16)}
                    exp_ss[0] = emit_exp(0)
                    exp_ss[1] = emit_exp(1)
                    emit_b(0)
                    emit_b(1)
                    for n in range(16):
                        emit_scan(n, exp_ss[n])
                        if n + 2 < 16:
                            exp_ss[n + 2] = emit_exp(n + 2)
                            emit_b(n + 2)
                        emit_z(n)
                        if ET < 15:
                            if n == 1:
                                pl["qv"] = prep_D(ET + 1)
                            elif n == 2:
                                pl["stv"] = prep_M(ET + 1)
                            elif n == 5:
                                pl["ssv"] = prep_E(ET + 1, pl["stv"],
                                                   pl["exp_end"],
                                                   pl["dxv_prev"])
                            elif n == 8:
                                pl["dxv"] = prep_X(ET + 1, pl["qv"],
                                                   pl["ssv"], pl["b15"])
                    P.wait("vector", "st", ymm_st[15])
                    P.wait("vector", "qS", pl.get("yq", 0))
                    for b2 in range(2):
                        P.emit("vector", lambda e, ET=ET, b2=b2, cur=cur:
                               e.scalar_tensor_tensor(
                                   out=yraw[:, b2 * 512:(b2 + 1) * 512],
                                   in0=x1t2B[cur][:, b2 * 512:(b2 + 1) * 512],
                                   scalar=vecs[:, 32 + ET:33 + ET],
                                   in1=pb[b2][:, :],
                                   op0=ALU.mult, op1=ALU.add), inc="sv")
                    P.wait("sync", "sv")
                    P.dma("sync", scr_yg3[0:128, ET, hs], yraw[:, :])
                    pl["yq"] = P.val("qS")
                    P.wait("tensor", "sv")
                    pl["scan14"] = scan_sv[14]
                    pl["scan15"] = scan_sv[15]
                    pl["exp_end"] = P.val("ss")
                    pl["b15"] = b_sv[15]
            skw = arb(0, 8192).rearrange("p (c n) -> p c n", n=2048)
            otw = arb(8192, 8192).rearrange("p (c n) -> p c n", n=1024)
            sskip = arb(16384, 4096).rearrange("p (c n) -> p c n", n=BLK)
            ygb = arb(20480, 4096).rearrange("p (c n) -> p c n", n=BLK)
            P.wait("gpsimd", "sv")
            P.wait("gpsimd", "ss")
            P.wait("gpsimd", "st")
            P.dma("gpsimd", skw, L0_skipT.rearrange("(c p) n -> p c n", p=128),
                  inc="qG")
            P.dma("gpsimd", otw, L0_outwT.rearrange("(c p) n -> p c n", p=128),
                  inc="qG")
            ldg = P.val("qG")
            P.wait("tensor", "qG", ldg)
            P.wait("sync", "ss")
            P.wait("sync", "st")
            for blk in range(NBLK):
                sl = slice(blk * BLK, blk * BLK + BLK)
                P.wait("sync", "sv")
                P.dma("sync", ygb[:, :, :],
                      scr_yg3[:, :, sl])
                P.dma("sync", xblk[:, :, :], scr_x3[:, :, sl])
                qv = P.val("qS")
                for ec in range(16):
                    bk = 2 + (ec % 4)
                    P.bank_wait(bk)
                    outp = pb[bk][:, :]
                    for kc in range(8):
                        t_mm(outp, skw[:, kc, ec * 128:(ec + 1) * 128],
                             xnbf[:, kc, sl], kc == 0, kc == 7,
                             inc="st" if kc == 7 else None)
                    stv = P.val("st")
                    P.wait("scalar", "st", stv)
                    P.emit("scalar", lambda e, ec=ec, outp=outp: e.activation(
                        out=sskip[:, ec, :], in_=outp, func=AF.Silu), inc="ss")
                    P.bank_set(bk, "ss")
                ssv = P.val("ss")
                P.wait("vector", "ss", ssv)
                P.wait("vector", "qS", qv)
                for hh in range(2):
                    hsl = slice(8 * hh, 8 * hh + 8)
                    P.emit("vector", lambda e, hsl=hsl: e.tensor_tensor(
                        out=ygb[:, hsl, :], in0=ygb[:, hsl, :],
                        in1=sskip[:, hsl, :], op=ALU.mult), inc="sv")
                svv = P.val("sv")
                P.wait("tensor", "sv", svv)
                for dc in range(8):
                    bk = dc % 2
                    P.bank_wait(bk)
                    outp = pb[bk][:, :]
                    for ec in range(16):
                        t_mm(outp, otw[:, ec, dc * 128:(dc + 1) * 128],
                             ygb[:, ec, :], ec == 0, ec == 15,
                             inc="st" if ec == 15 else None)
                    stv = P.val("st")
                    P.wait("vector", "st", stv)
                    P.emit("vector", lambda e, dc=dc, outp=outp: e.tensor_tensor(
                        out=xblk[:, dc, :], in0=outp, in1=xblk[:, dc, :],
                        op=ALU.add), inc="sv")
                    P.bank_set(bk, "sv")
                P.wait("sync", "sv")
                P.dma("sync", scr_x3[:, :, sl], xblk[:, :, :])
            P.wait("sync", "qS")

        last = max(ENABLE) if ENABLE else 0
        if 1 in ENABLE:
            ln_stats(xsrc=x_in3)
            xn_build(xsrc=x_in3)
            ffn_precompute()
            ffn_apply(xsrc=x_in3)

        if 2 in ENABLE:
            ln_stats(want_sl2=True)
            xn_build(0, 1, use_nw=True)
            P.wait("sync", "sv")
            P.dma("sync", rows[0:1, 0, :], st16[:, 4, :])
            P.dma("sync", rows[0:1, 1, :], st16[:, 5, :])
            xn_build(0, 1, xprime_only=True)
            mamba_apply()

        if 3 in ENABLE:
            ln_stats()
            xn_build()
            ffn_apply()

        if 4 in ENABLE:
            ln_stats()
            rwkv_apply()

        if 5 in ENABLE:
            ln_stats()
            xn_build(roll=True)
            swin_apply()

        if 6 in ENABLE:
            ln_stats()
            rwkv_apply(out_dst=out3 if last == 6 else None)

        if last != 6:
            for h in range(NBLK):
                sl = slice(h * BLK, h * BLK + BLK)
                P.dma("sync", xblk[:, :, :], scr_x3[:, :, sl])
                P.wait("sync", "qS")
                P.dma("sync", out3[:, :, sl], xblk[:, :, :])
                P.wait("sync", "qS")
        P.wait("sync", "qS")
        P.barrier()

        block = EE(nc.Block())

        @block.sync
        def _(eng):
            for fn in P.ops["sync"]:
                fn(eng)

        @block.gpsimd
        def _(eng):
            for fn in P.ops["gpsimd"]:
                fn(eng)

        @block.scalar
        def _(eng):
            for fn in P.ops["scalar"]:
                fn(eng)

        @block.vector
        def _(eng):
            for fn in P.ops["vector"]:
                fn(eng)

        @block.tensor
        def _(eng):
            for fn in P.ops["tensor"]:
                fn(eng)

    return nc


def host_prep(inputs):
    f32 = np.float32
    a = np.arange(D, dtype=np.float64)
    ang = 2 * np.pi / D * np.outer(a, a)
    Cb = np.cos(ang).astype(NPBF)
    Sb = np.sin(ang).astype(NPBF)

    WS = 8
    coords = np.stack(np.meshgrid(np.arange(WS), np.arange(WS), indexing="ij"))
    cf = coords.reshape(2, -1)
    rel = cf[:, :, None] - cf[:, None, :]
    rel = rel.transpose(1, 2, 0).astype(np.int64)
    rel[..., 0] += WS - 1
    rel[..., 1] += WS - 1
    rel[..., 0] *= 2 * WS - 1
    idx8 = rel.sum(-1)[:WS, :WS]
    OH = np.zeros((225, 64), NPBF)
    for aa in range(8):
        for b in range(8):
            OH[idx8[aa, b], aa * 8 + b] = 1.0
    E8 = np.zeros((8, 128), NPBF)
    for q in range(128):
        E8[q % 8, q] = 1.0
    E16 = np.zeros((16, 128), NPBF)
    for q in range(128):
        E16[q // 8, q] = 1.0
    BK = np.full((16, 128), -32768.0, NPBF)
    for w in range(16):
        BK[w, w * 8:w * 8 + 8] = 0.0
    IDN = np.eye(128, dtype=NPBF)
    SELRh = np.zeros((16, 16, 128), NPBF)
    for n in range(16):
        SELRh[n, n, :] = 1.0

    shared = {
        "CC": Cb, "SS": Sb,
        "FF_Wr": np.ascontiguousarray(inputs["L0_ffn_w_freq_real"], f32),
        "FF_Wi": np.ascontiguousarray(inputs["L0_ffn_w_freq_imag"], f32),
        "FF_bias": np.ascontiguousarray(inputs["L0_ffn_bias"], f32),
        "L0_inwT": np.ascontiguousarray(np.asarray(inputs["L0_attn_in_w"], f32).T),
        "L0_skipT": np.ascontiguousarray(np.asarray(inputs["L0_attn_skip_w"], f32).T),
        "L0_outwT": np.ascontiguousarray(np.asarray(inputs["L0_attn_out_w"], f32).T),
        "L0_d1bcT": np.ascontiguousarray(np.concatenate(
            [np.asarray(inputs["L0_attn_wd1"], f32).T,
             np.asarray(inputs["L0_attn_wb"], f32).T,
             np.asarray(inputs["L0_attn_wc"], f32).T], axis=1)),
        "L0_wd2T": np.ascontiguousarray(np.asarray(inputs["L0_attn_wd2"], f32).T),
        "L0_Alog": np.ascontiguousarray(inputs["L0_attn_A_log"], f32),
        "L0_convw": np.ascontiguousarray(inputs["L0_attn_conv_w"], f32),
        "L0_convb": np.ascontiguousarray(inputs["L0_attn_conv_b"], f32),
        "L0_wd2b": np.ascontiguousarray(inputs["L0_attn_wd2_b"], f32),
        "L0_WD": np.ascontiguousarray(inputs["L0_attn_W_D"], f32),
        "L0_normw": np.ascontiguousarray(inputs["L0_attn_norm_w"], f32),
        "Q_wqkv": np.ascontiguousarray(inputs["L1_attn_w_qkv"], f32),
        "Q_bqkv": np.ascontiguousarray(inputs["L1_attn_b_qkv"], f32),
        "Q_wproj": np.ascontiguousarray(inputs["L1_attn_w_proj"], f32),
        "Q_bproj": np.ascontiguousarray(inputs["L1_attn_b_proj"], f32),
        "Q_btab": np.ascontiguousarray(inputs["L1_attn_bias_table"], f32),
        "Q_OH": OH, "E8c": E8, "E16w": E16, "BKw": BK, "IDN": IDN,
        "SELR": SELRh.transpose(1, 0, 2) if False else SELRh,
        "W1_WrT": np.ascontiguousarray(np.asarray(inputs["L1_ffn_W_r"], f32).T),
        "W1_WkT": np.ascontiguousarray(np.asarray(inputs["L1_ffn_W_k"], f32).T),
        "W1_WvT": np.ascontiguousarray(np.asarray(inputs["L1_ffn_W_v"], f32).T),
        "W1_mur": np.ascontiguousarray(inputs["L1_ffn_mu_r"], f32),
        "W1_muk": np.ascontiguousarray(inputs["L1_ffn_mu_k"], f32),
    }
    resid = np.asarray(inputs["L0_attn_resid"], f32)
    in_maps = []
    for c in range(NC):
        m = dict(shared)
        m["x_in"] = np.ascontiguousarray(resid[c].T)
        in_maps.append(m)
    return in_maps


def kernel(**inputs):
    in_maps = host_prep(inputs)
    nc = build_program()
    res = run_bass_kernel_spmd(nc, in_maps, core_ids=list(range(NC)))
    outs = [np.ascontiguousarray(np.asarray(r["out"]).T) for r in res.results]
    return np.stack(outs, axis=0)


# version 26 (final)
# speedup vs baseline: 1.0112x; 1.0112x over previous
import contextlib

import numpy as np
import ml_dtypes

import concourse.bass as bass
import concourse.mybir as mybir
from concourse.bass_utils import run_bass_kernel_spmd

F32 = mybir.dt.float32
BF16 = mybir.dt.bfloat16
AF = mybir.ActivationFunctionType
ALU = mybir.AluOpType
NPBF = ml_dtypes.bfloat16

D = 1024
T = 2048
E = 2048
NC = 8
NBLK = 4
BLK = 512
EPS = 1e-5

ENABLE = (1, 2, 3, 4, 5, 6)

ENGS = ("sync", "gpsimd", "scalar", "vector", "tensor")
SEM_OF = {"sync": "sy", "gpsimd": "sg", "scalar": "ss",
          "vector": "sv", "tensor": "st"}
BSEM_OF = {"sync": "bsy", "gpsimd": "bsg", "scalar": "bss",
           "vector": "bsv", "tensor": "bst"}


class Prog:

    def __init__(self, sems):
        self.ops = {e: [] for e in ENGS}
        self.sems = sems
        self.cnt = {k: 0 for k in sems}

    def emit(self, eng, fn, inc=None, inc_n=1):
        if inc is None:
            self.ops[eng].append(fn)
            return
        h = self.sems[inc]
        self.cnt[inc] += inc_n

        def wrapped(e, fn=fn, h=h, n=inc_n):
            fn(e).then_inc(h, n)
        self.ops[eng].append(wrapped)

    def dma(self, eng, out, in_, inc="qS", slow=False):
        h = self.sems[inc]
        pre = self.cnt[inc]
        self.ops[eng].append(lambda e, h=h, v=pre: e.wait_ge(h, v))
        self.cnt[inc] += 16

        def fn(e, out=out, in_=in_, h=h, slow=slow):
            e.dma_start(out=out, in_=in_,
                        allow_slow_non_contiguous=slow).then_inc(h, 16)
        self.ops[eng].append(fn)

    def val(self, sem):
        return self.cnt[sem]

    def wait(self, eng, sem, val=None):
        v = self.cnt[sem] if val is None else val
        h = self.sems[sem]
        self.ops[eng].append(lambda e, h=h, v=v: e.wait_ge(h, v))

    def drain(self, eng):
        self.ops[eng].append(lambda e: e.drain())

    def fence(self, eng):
        self.wait(eng, SEM_OF[eng])

    def bank_wait(self, i):
        if i in getattr(self, "bank_last", {}):
            sem, v = self.bank_last[i]
            self.wait("tensor", sem, v)

    def bank_set(self, i, sem):
        if not hasattr(self, "bank_last"):
            self.bank_last = {}
        self.bank_last[i] = (sem, self.val(sem))

    def barrier(self, engines=ENGS):
        for e in engines:
            s = BSEM_OF[e]
            h = self.sems[s]
            self.cnt[s] += 1
            self.ops[e].append(lambda en, h=h: en.sem_inc(h, 1))
        for e in engines:
            for o in engines:
                if o != e:
                    self.wait(e, BSEM_OF[o])


def build_program():
    nc = bass.Bass("TRN2", target_bir_lowering=False)

    def din(name, shape, dt=F32):
        return nc.dram_tensor(name, shape, dt, kind="ExternalInput").ap()

    x_in = din("x_in", [D, T])
    out_t = nc.dram_tensor("out", [D, T], F32, kind="ExternalOutput").ap()

    CC = din("CC", [D, D], BF16)
    SS = din("SS", [D, D], BF16)
    FF_Wr = din("FF_Wr", [D, D])
    FF_Wi = din("FF_Wi", [D, D])
    FF_bias = din("FF_bias", [D])
    L0_inwT = din("L0_inwT", [D, E])
    L0_skipT = din("L0_skipT", [D, E])
    L0_outwT = din("L0_outwT", [E, D])
    L0_d1bcT = din("L0_d1bcT", [E, 96])
    L0_wd2T = din("L0_wd2T", [64, E])
    L0_Alog = din("L0_Alog", [E, 16])
    L0_convw = din("L0_convw", [E, 4])
    L0_convb = din("L0_convb", [E])
    L0_wd2b = din("L0_wd2b", [E])
    L0_WD = din("L0_WD", [E])
    VECC = din("VECC", [128, 104])
    L0_normw = din("L0_normw", [D])
    Q_wqkv = din("Q_wqkv", [D, 3 * D])
    Q_bqkv = din("Q_bqkv", [3 * D])
    Q_wproj = din("Q_wproj", [D, D])
    Q_bproj = din("Q_bproj", [D])
    Q_btab = din("Q_btab", [225, 16])
    Q_OH = din("Q_OH", [225, 64], BF16)
    E8c = din("E8c", [8, 128], BF16)
    E16w = din("E16w", [16, 128], BF16)
    BKw = din("BKw", [16, 128], BF16)
    IDN = din("IDN", [128, 128], BF16)
    W1_WrT = din("W1_WrT", [D, D])
    W1_WkT = din("W1_WkT", [D, D])
    W1_WvT = din("W1_WvT", [D, D])
    W1_mur = din("W1_mur", [D])
    SELR = din("SELR", [16, 16, 128], BF16)
    W1_muk = din("W1_muk", [D])

    scr_x = nc.dram_tensor("scr_x", [D, T], F32, kind="Internal").ap()
    scr_x1 = nc.dram_tensor("scr_x1", [E, T], BF16, kind="Internal").ap()
    scr_yg = nc.dram_tensor("scr_yg", [E, T], BF16, kind="Internal").ap()
    scr_M = nc.dram_tensor("scr_M", [128, 16384], BF16, kind="Internal").ap()
    scr_sm = nc.dram_tensor("scr_sm", [64, 16], BF16, kind="Internal").ap()
    scr_po = nc.dram_tensor("scr_po", [D, T], F32, kind="Internal").ap()

    x_in3 = x_in.rearrange("(c p) t -> p c t", p=128)
    scr_x3 = scr_x.rearrange("(c p) t -> p c t", p=128)
    out3 = out_t.rearrange("(c p) t -> p c t", p=128)
    scr_po3 = scr_po.rearrange("(c p) t -> p c t", p=128)

    with contextlib.ExitStack() as st_:
        EE = st_.enter_context

        AR_N = 24576
        arena = EE(nc.sbuf_tensor("arena", [128, AR_N], F32))

        def arf(off, n):
            assert off >= 0 and off + n <= AR_N, (off, n)
            return arena[:, off:off + n]

        def arb(off, n):
            assert off >= 0 and off + n <= AR_N, (off, n)
            return arena[:, off:off + n].bitcast(BF16)

        xnbf = EE(nc.sbuf_tensor("xnbf", [128, 8, T], BF16))
        xblk = EE(nc.sbuf_tensor("xblk", [128, 8, BLK], F32))
        tmp512 = EE(nc.sbuf_tensor("tmp512", [128, BLK], F32))
        b1f = EE(nc.sbuf_tensor("b1f", [128, BLK], F32))
        tmp2 = EE(nc.sbuf_tensor("tmp2", [128, 2, BLK], F32))
        b2f = EE(nc.sbuf_tensor("b2f", [128, BLK], F32))
        rows = EE(nc.sbuf_tensor("rows", [1, 2, T], F32))
        st16 = EE(nc.sbuf_tensor("st16", [128, 12, 16], F32))
        ones_b = EE(nc.sbuf_tensor("ones_b", [128, 1], BF16))
        ones_f1 = EE(nc.sbuf_tensor("ones_f1", [1, 128], F32))
        onesb_1 = EE(nc.sbuf_tensor("onesb_1", [1, 128], BF16))
        nwrow = EE(nc.sbuf_tensor("nwrow", [1, D], F32))
        vecs = EE(nc.sbuf_tensor("vecs", [128, 64], F32))
        Aneg = EE(nc.sbuf_tensor("Aneg", [128, 16, 16], F32))
        qb_sc = EE(nc.sbuf_tensor("qb_sc", [128, 24], F32))
        pbias = EE(nc.sbuf_tensor("pbias", [128, 8], F32))
        murc = EE(nc.sbuf_tensor("murc", [128, 8], F32))
        mukc = EE(nc.sbuf_tensor("mukc", [128, 8], F32))
        e8_sb = EE(nc.sbuf_tensor("e8_sb", [8, 128], BF16))
        e16_sb = EE(nc.sbuf_tensor("e16_sb", [16, 128], BF16))
        bkw_sb = EE(nc.sbuf_tensor("bkw_sb", [16, 128], BF16))
        idn_sb = EE(nc.sbuf_tensor("idn_sb", [128, 128], BF16))
        oh_a = EE(nc.sbuf_tensor("oh_a", [128, 64], BF16))
        oh_b = EE(nc.sbuf_tensor("oh_b", [97, 64], BF16))
        bt_a = EE(nc.sbuf_tensor("bt_a", [128, 16], BF16))
        bt_b = EE(nc.sbuf_tensor("bt_b", [97, 16], BF16))
        bt_af = EE(nc.sbuf_tensor("bt_af", [128, 16], F32))
        bt_bf = EE(nc.sbuf_tensor("bt_bf", [97, 16], F32))
        bias8 = EE(nc.sbuf_tensor("bias8", [64, 16], BF16))
        bcolb = EE(nc.sbuf_tensor("bcolb", [8, 8, 16], BF16))
        bcol = EE(nc.sbuf_tensor("bcol", [8, 16, 16, 8], BF16))
        convw_sb = EE(nc.sbuf_tensor("convw_sb", [128, 16, 4], F32))
        d1bc_sb = EE(nc.sbuf_tensor("d1bc_sb", [128, 16, 96], BF16))
        wd2_sb = EE(nc.sbuf_tensor("wd2_sb", [64, 16, 128], BF16))
        hcar = EE(nc.sbuf_tensor("hcar", [128, 16, 16], F32))
        csts = EE(nc.sbuf_tensor("csts", [128, 4], F32))
        qbv_row = EE(nc.sbuf_tensor("qbv_row", [1, D], BF16))
        PbfB = [EE(nc.sbuf_tensor("PbfA", [128, 128], F32)),
                EE(nc.sbuf_tensor("PbfBt", [128, 128], F32))]
        PbfPB = [EE(nc.sbuf_tensor("PbfPA", [128, 128], BF16)),
                 EE(nc.sbuf_tensor("PbfPBt", [128, 128], BF16))]
        PtB = [EE(nc.sbuf_tensor("PtA", [128, 128], BF16)),
               EE(nc.sbuf_tensor("PtBt", [128, 128], BF16))]
        selr_sb = EE(nc.sbuf_tensor("selr_sb", [16, 16, 128], BF16))
        bmt = EE(nc.sbuf_tensor("bmt", [16, T // 2], BF16))
        cmt = EE(nc.sbuf_tensor("cmt", [16, T // 2], BF16))

        pb = [EE(nc.psum_tensor(f"pb{i}", [128, 512], F32)) for i in range(7)]
        ptrb = EE(nc.psum_tensor("ptrb", [128, 1024], BF16))

        sems = {}
        for s in ("qS", "qG", "sv", "ss", "st", "sg", "sy",
                  "bsy", "bsg", "bss", "bsv", "bst"):
            sems[s] = EE(nc.semaphore(s))
        P = Prog(sems)
        xcb = arena[:, AR_N - 4096:AR_N - 2048].bitcast(BF16)\
            .rearrange("p (c b) -> p c b", b=BLK)
        xsqb = arena[:, AR_N - 2048:AR_N].bitcast(BF16)\
            .rearrange("p (c b) -> p c b", b=BLK)
        s1row = rows[:, 0, :]
        s2row = rows[:, 1, :]
        b1s = arf(8192, 4096).rearrange("p (c b) -> p c b", b=BLK)
        b2s = arf(12288, 4096).rearrange("p (c b) -> p c b", b=BLK)
        tmp8 = arf(16384, 4096).rearrange("p (c b) -> p c b", b=BLK)

        def t_mm(out, lhsT, rhs, start, stop, inc=None):
            def fn(e, out=out, lhsT=lhsT, rhs=rhs, start=start, stop=stop):
                return e.matmul(out=out, lhsT=lhsT, rhs=rhs, start=start,
                                stop=stop)
            P.emit("tensor", fn, inc=inc)

        P.dma("sync", e8_sb[:, :], E8c[:, :])
        P.dma("sync", e16_sb[:, :], E16w[:, :])
        P.dma("sync", bkw_sb[:, :], BKw[:, :])
        P.dma("gpsimd", idn_sb[:, :], IDN[:, :], inc="qG")
        P.dma("sync", oh_a[:, :], Q_OH[0:128, :])
        P.dma("sync", oh_b[:, :], Q_OH[128:225, :])
        P.dma("gpsimd", bt_af[:, :], Q_btab[0:128, :], inc="qG")
        P.dma("gpsimd", bt_bf[:, :], Q_btab[128:225, :], inc="qG")
        P.dma("sync", nwrow[0:1, :], L0_normw.unsqueeze(0))
        P.dma("sync", vecs[:, 0:56], VECC[:, 0:56])
        P.dma("gpsimd", Aneg[:, :, :],
              L0_Alog.rearrange("(c p) n -> p c n", p=128), inc="qG")
        P.dma("sync", qb_sc[:, :], VECC[:, 56:80])
        P.dma("gpsimd", qbv_row[0:1, :], Q_bqkv[2048:3072].unsqueeze(0), inc="qG")
        P.dma("gpsimd", selr_sb[:, :, :], SELR[:, :, :], inc="qG")
        P.dma("sync", pbias[:, :], VECC[:, 80:88])
        P.dma("sync", murc[:, :], VECC[:, 88:96])
        P.dma("sync", mukc[:, :], VECC[:, 96:104])
        P.dma("gpsimd", convw_sb[:, :, :],
              L0_convw.rearrange("(c p) k -> p c k", p=128), inc="qG")
        P.dma("gpsimd", d1bc_sb[:, :, :],
              L0_d1bcT.rearrange("(c p) k -> p c k", p=128), inc="qG")
        P.dma("gpsimd", wd2_sb[:, :, :],
              L0_wd2T.rearrange("r (c p) -> r c p", p=128), inc="qG")
        init_loads = P.val("qS")
        init_loads_g = P.val("qG")

        P.wait("vector", "qS", init_loads)
        P.wait("vector", "qG", init_loads_g)
        P.emit("vector", lambda e: e.memset(csts[:, 0:1], EPS))
        P.emit("vector", lambda e: e.memset(csts[:, 1:2], 0.5 / D))
        P.emit("vector", lambda e: e.memset(csts[:, 2:3], 0.125))
        P.emit("vector", lambda e: e.memset(ones_b[:], 1.0))
        P.emit("vector", lambda e: e.memset(ones_f1[:], 1.0))
        P.emit("vector", lambda e: e.memset(onesb_1[:], 1.0))
        P.emit("vector", lambda e: e.tensor_scalar(
            out=vecs[:, 48:56], in0=vecs[:, 48:56], scalar1=0.5, scalar2=None,
            op0=ALU.mult))
        P.emit("vector", lambda e: e.tensor_scalar(
            out=qb_sc[:, 0:8], in0=qb_sc[:, 0:8], scalar1=0.125, scalar2=None,
            op0=ALU.mult), inc="sv")
        P.wait("scalar", "qS", init_loads)
        P.wait("scalar", "qG", init_loads_g)
        P.emit("scalar", lambda e: e.activation(
            out=Aneg[:, :, :], in_=Aneg[:, :, :], func=AF.Exp))
        P.emit("scalar", lambda e: e.copy(out=bt_a[:], in_=bt_af[:]))
        P.emit("scalar", lambda e: e.copy(out=bt_b[:], in_=bt_bf[:]), inc="ss")
        P.wait("vector", "ss")
        P.emit("vector", lambda e: e.tensor_scalar(
            out=Aneg[:, :, :], in0=Aneg[:, :, :], scalar1=-1.0, scalar2=None,
            op0=ALU.mult), inc="sv")
        P.wait("tensor", "ss")
        t_mm(pb[0][0:64, 0:16], oh_a[:, :], bt_a[:, :], True, False)
        t_mm(pb[0][0:64, 0:16], oh_b[:, :], bt_b[:, :], False, True, inc="st")
        P.wait("scalar", "st")
        P.emit("scalar", lambda e: e.copy(out=bias8[:, :], in_=pb[0][0:64, 0:16]),
               inc="ss")
        P.wait("sync", "ss")
        P.dma("sync", scr_sm[:, :], bias8[:, :])
        P.wait("sync", "qS")
        P.dma("sync", bcolb[:, :, :],
              scr_sm.rearrange("(a b) h -> a b h", b=8))
        P.wait("vector", "qS")
        P.emit("vector", lambda e: e.tensor_copy(
            out=bcol[:, :, :, :],
            in_=bcolb.rearrange("a b h -> a h b").unsqueeze(2)
                .broadcast_to((8, 16, 16, 8))), inc="sv")
        P.barrier()

        xbB = [xblk, arf(0, 4096).rearrange("p (c b) -> p c b", b=BLK)]

        def ln_stats(want_sl2=False, xsrc=None):
            xs = scr_x3 if xsrc is None else xsrc
            con_sv = {}
            con_ss = {}
            qvs = {}

            def lnst_dma(blk):
                if blk < 2:
                    P.wait("sync", "sv")
                    P.wait("sync", "ss")
                    P.wait("sync", "st")
                else:
                    P.wait("sync", "sv", con_sv[blk - 2])
                    P.wait("sync", "ss", con_ss[blk - 2])
                P.dma("sync", xbB[blk % 2][:, :, :],
                      xs[:, :, blk * BLK:(blk + 1) * BLK])
                qvs[blk] = P.val("qS")

            lnst_dma(0)
            lnst_dma(1)
            for blk in range(NBLK):
                sl = slice(blk * BLK, blk * BLK + BLK)
                xb = xbB[blk % 2]
                P.wait("vector", "qS", qvs[blk])
                P.wait("vector", "st")
                P.emit("vector", lambda e, xb=xb: e.tensor_copy(
                    out=xcb[:, :, :], in_=xb[:, :, :]), inc="sv")
                con_sv[blk] = P.val("sv")
                P.wait("scalar", "qS", qvs[blk])
                P.wait("scalar", "st")
                P.emit("scalar", lambda e, xb=xb: e.activation(
                    out=xsqb[:, :, :], in_=xb[:, :, :], func=AF.Square),
                    inc="ss")
                con_ss[blk] = P.val("ss")
                P.wait("tensor", "sv")
                P.wait("tensor", "ss")
                P.bank_wait(0)
                P.bank_wait(1)
                for c in range(8):
                    t_mm(pb[0][0:1, 0:BLK], ones_b[:, :], xcb[:, c, :],
                         c == 0, c == 7)
                for c in range(8):
                    t_mm(pb[1][0:1, 0:BLK], ones_b[:, :], xsqb[:, c, :],
                         c == 0, c == 7, inc="st" if c == 7 else None)
                P.wait("scalar", "st")
                P.emit("scalar", lambda e, sl=sl: e.copy(
                    out=s1row[:, sl], in_=pb[0][0:1, 0:BLK]))
                P.emit("scalar", lambda e, sl=sl: e.copy(
                    out=s2row[:, sl], in_=pb[1][0:1, 0:BLK]), inc="ss")
                P.bank_set(0, "ss")
                P.bank_set(1, "ss")
                if blk + 2 < NBLK:
                    lnst_dma(blk + 2)
            P.wait("sync", "ss")
            P.dma("sync", st16[:, 0, :], s1row[:, :])
            P.dma("sync", st16[:, 1, :], s2row[:, :])
            qv = P.val("qS")
            P.wait("vector", "qS", qv)
            sc = 1.0 / D
            P.emit("vector", lambda e: e.tensor_scalar(
                out=st16[:, 2, :], in0=st16[:, 0, :], scalar1=sc, scalar2=None,
                op0=ALU.mult), inc="sv")
            P.fence("vector")
            P.emit("vector", lambda e: e.tensor_tensor(
                out=st16[:, 6, :], in0=st16[:, 2, :], in1=st16[:, 2, :],
                op=ALU.mult), inc="sv")
            P.fence("vector")
            P.emit("vector", lambda e: e.scalar_tensor_tensor(
                out=st16[:, 3, :], in0=st16[:, 1, :], scalar=sc,
                in1=st16[:, 6, :], op0=ALU.mult, op1=ALU.subtract), inc="sv")
            P.wait("scalar", "sv")
            P.emit("scalar", lambda e: e.activation(
                out=st16[:, 6, :], in_=st16[:, 3, :], func=AF.Sqrt, bias=csts[:, 0:1]),
                inc="ss")
            P.wait("vector", "ss")
            P.emit("vector", lambda e: e.reciprocal(
                out=st16[:, 4, :], in_=st16[:, 6, :]), inc="sv")
            P.fence("vector")
            P.emit("vector", lambda e: e.tensor_tensor(
                out=st16[:, 5, :], in0=st16[:, 2, :], in1=st16[:, 4, :],
                op=ALU.mult), inc="sv")
            if want_sl2:
                P.emit("vector", lambda e: e.tensor_tensor(
                    out=st16[:, 6, :], in0=st16[:, 3, :], in1=st16[:, 4, :],
                    op=ALU.mult), inc="sv")
                P.fence("vector")
                P.emit("vector", lambda e: e.tensor_tensor(
                    out=st16[:, 6, :], in0=st16[:, 6, :], in1=st16[:, 4, :],
                    op=ALU.mult), inc="sv")
                P.wait("scalar", "sv")
                P.emit("scalar", lambda e: e.activation(
                    out=st16[:, 7, :], in_=st16[:, 6, :], func=AF.Sqrt,
                    bias=csts[:, 0:1]), inc="ss")
                P.wait("vector", "ss")
                P.emit("vector", lambda e: e.reciprocal(
                    out=st16[:, 7, :], in_=st16[:, 7, :]), inc="sv")
                P.fence("vector")
                P.emit("vector", lambda e: e.tensor_tensor(
                    out=st16[:, 8, :], in0=st16[:, 4, :], in1=st16[:, 7, :],
                    op=ALU.mult), inc="sv")
                P.fence("vector")
                P.emit("vector", lambda e: e.tensor_tensor(
                    out=st16[:, 9, :], in0=st16[:, 2, :], in1=st16[:, 8, :],
                    op=ALU.mult), inc="sv")
            P.wait("sync", "sv")
            if want_sl2:
                P.dma("sync", rows[0:1, 0, :], st16[:, 8, :])
                P.dma("sync", rows[0:1, 1, :], st16[:, 9, :])
            else:
                P.dma("sync", rows[0:1, 0, :], st16[:, 4, :])
                P.dma("sync", rows[0:1, 1, :], st16[:, 5, :])
            P.wait("vector", "qS")
            P.wait("tensor", "qS")

        def xn_build(r_idx=0, mr_idx=1, use_nw=False, sl2=False, roll=False,
                     xprime_only=False, xsrc=None):
            xs = scr_x3 if xsrc is None else xsrc
            con_sv = {}
            qvs = {}

            def xnb_dma(blk):
                if blk < 2:
                    P.wait("sync", "sv")
                    P.wait("sync", "ss")
                else:
                    P.wait("sync", "sv", con_sv[blk - 2])
                P.wait("sync", "qS")
                P.dma("sync", xbB[blk % 2][:, :, :],
                      xs[:, :, blk * BLK:(blk + 1) * BLK])
                qvs[blk] = P.val("qS")

            xnb_dma(0)
            xnb_dma(1)
            for blk in range(NBLK):
                sl = slice(blk * BLK, blk * BLK + BLK)
                xb = xbB[blk % 2]
                qv = qvs[blk]
                P.wait("tensor", "qS", qv)
                if not use_nw:
                    P.bank_wait(2)
                    P.bank_wait(3)
                    t_mm(pb[2][:, 0:BLK], ones_f1[:, :], rows[0:1, r_idx, sl],
                         True, True)
                    t_mm(pb[3][:, 0:BLK], ones_f1[:, :], rows[0:1, mr_idx, sl],
                         True, True, inc="st")
                if use_nw:
                    for c in range(8):
                        P.bank_wait(2)
                        P.bank_wait(3)
                        t_mm(pb[2][:, 0:BLK], nwrow[0:1, c * 128:(c + 1) * 128],
                             rows[0:1, r_idx, sl], True, True)
                        t_mm(pb[3][:, 0:BLK], nwrow[0:1, c * 128:(c + 1) * 128],
                             rows[0:1, mr_idx, sl], True, True, inc="st")
                        stv = P.val("st")
                        P.wait("scalar", "st", stv)
                        P.emit("scalar", lambda e, c=c: e.copy(
                            out=b1s[:, c, :], in_=pb[2][:, 0:BLK]))
                        P.emit("scalar", lambda e, c=c: e.copy(
                            out=b2s[:, c, :], in_=pb[3][:, 0:BLK]), inc="ss")
                        P.bank_set(2, "ss")
                        P.bank_set(3, "ss")
                    b1v = b1s[:, :, :]
                    b2v = b2s[:, :, :]
                    ssv = P.val("ss")
                    P.wait("vector", "ss", ssv)
                else:
                    stv = P.val("st")
                    P.wait("scalar", "st", stv)
                    P.emit("scalar", lambda e: e.copy(
                        out=b1f[:, :], in_=pb[2][:, 0:BLK]))
                    P.emit("scalar", lambda e: e.copy(
                        out=b2f[:, :], in_=pb[3][:, 0:BLK]), inc="ss")
                    P.bank_set(2, "ss")
                    P.bank_set(3, "ss")
                    ssv = P.val("ss")
                    P.wait("vector", "ss", ssv)
                    b1v = b1f[:, :].unsqueeze(1).broadcast_to((128, 8, BLK))
                    b2v = b2f[:, :].unsqueeze(1).broadcast_to((128, 8, BLK))
                P.wait("vector", "qS", qv)
                t1_sv = {}
                for hh in range(2):
                    hsl = slice(4 * hh, 4 * hh + 4)
                    P.emit("vector", lambda e, b1v=b1v, hsl=hsl, xb=xb:
                           e.tensor_tensor(
                               out=tmp8[:, hsl, :], in0=xb[:, hsl, :],
                               in1=b1v[:, hsl, :], op=ALU.mult), inc="sv")
                    t1_sv[hh] = P.val("sv")
                if xprime_only:
                    t2_sv = {}
                    for hh in range(2):
                        hsl = slice(4 * hh, 4 * hh + 4)
                        P.wait("vector", "sv", t1_sv[hh])
                        P.emit("vector", lambda e, b2v=b2v, hsl=hsl: e.tensor_tensor(
                            out=tmp8[:, hsl, :], in0=tmp8[:, hsl, :],
                            in1=b2v[:, hsl, :], op=ALU.subtract), inc="sv")
                        t2_sv[hh] = P.val("sv")
                    for hh in range(2):
                        hsl = slice(4 * hh, 4 * hh + 4)
                        P.wait("vector", "sv", t2_sv[hh])
                        P.emit("vector", lambda e, hsl=hsl, xb=xb: e.tensor_tensor(
                            out=xb[:, hsl, :], in0=xb[:, hsl, :],
                            in1=tmp8[:, hsl, :], op=ALU.add), inc="sv")
                elif not roll:
                    for hh in range(2):
                        hsl = slice(4 * hh, 4 * hh + 4)
                        P.wait("vector", "sv", t1_sv[hh])
                        P.emit("vector", lambda e, sl=sl, b2v=b2v, hsl=hsl:
                               e.tensor_tensor(
                                   out=xnbf[:, hsl, sl], in0=tmp8[:, hsl, :],
                                   in1=b2v[:, hsl, :], op=ALU.subtract),
                               inc="sv")
                elif blk == 0:
                    P.wait("vector", "sv", t1_sv[1])
                    P.emit("vector", lambda e, b2v=b2v: e.tensor_tensor(
                        out=xnbf[:, :, T - 4:T], in0=tmp8[:, :, 0:4],
                        in1=b2v[:, :, 0:4], op=ALU.subtract))
                    P.emit("vector", lambda e, b2v=b2v: e.tensor_tensor(
                        out=xnbf[:, :, 0:508], in0=tmp8[:, :, 4:512],
                        in1=b2v[:, :, 4:512], op=ALU.subtract), inc="sv")
                else:
                    P.wait("vector", "sv", t1_sv[1])
                    P.emit("vector", lambda e, blk=blk, b2v=b2v: e.tensor_tensor(
                        out=xnbf[:, :, blk * BLK - 4:blk * BLK + 508],
                        in0=tmp8[:, :, :], in1=b2v, op=ALU.subtract), inc="sv")
                P.wait("tensor", "sv")
                con_sv[blk] = P.val("sv")
                if xprime_only:
                    P.wait("sync", "sv")
                    P.dma("sync", scr_x3[:, :, sl], xb[:, :, :])
                if blk + 2 < NBLK:
                    xnb_dma(blk + 2)
            P.wait("vector", "qS")

        def ffn_precompute():
            Cb = arb(0, 4096).rearrange("p (c m) -> p c m", m=1024)
            Sb = arb(4096, 4096).rearrange("p (c m) -> p c m", m=1024)
            Wr_b = arb(8192, 4096).rearrange("p (c o) -> p c o", o=1024)
            Wi_b = arb(12288, 4096).rearrange("p (c o) -> p c o", o=1024)
            PTf = arb(16384, 4096)
            QnTf = arb(20480, 4096)
            PTb = PTf.rearrange("p (c d) -> p c d", d=1024)
            QnTb = QnTf.rearrange("p (c d) -> p c d", d=1024)
            Mhif = arb(8192, 4096)
            Mlof = arb(12288, 4096)
            stg = arf(16384, 2048)

            P.wait("sync", "sv")
            P.wait("sync", "ss")
            P.dma("sync", Cb, CC.rearrange("(c p) m -> p c m", p=128))
            P.dma("sync", Sb, SS.rearrange("(c p) m -> p c m", p=128))
            for (src, dstf) in ((FF_Wr, arb(8192, 4096)), (FF_Wi, arb(12288, 4096))):
                for h in range(4):
                    P.dma("sync", stg.rearrange("p (a n) -> p a n", n=1024),
                          src.rearrange("(c p) o -> p c o", p=128)[:, 2 * h:2 * h + 2, :])
                    qv = P.val("qS")
                    P.wait("vector", "qS", qv)
                    P.emit("vector", lambda e, dstf=dstf, h=h: e.tensor_copy(
                        out=dstf[:, h * 2048:h * 2048 + 2048], in_=stg),
                        inc="sv")
                    P.wait("sync", "sv")

            def stage_AB(dstf, w1, r1, w2, r2):
                for oc in range(8):
                    for dh in range(2):
                        bk = 2 + (2 * oc + dh) % 4
                        P.bank_wait(bk)
                        outp = pb[bk][:, :]
                        for bc in range(8):
                            t_mm(outp, w1[:, bc, oc * 128:(oc + 1) * 128],
                                 r1[:, bc, dh * 512:dh * 512 + 512],
                                 bc == 0, False)
                        for bc in range(8):
                            t_mm(outp, w2[:, bc, oc * 128:(oc + 1) * 128],
                                 r2[:, bc, dh * 512:dh * 512 + 512],
                                 False, bc == 7, inc="st" if bc == 7 else None)
                        stv = P.val("st")
                        P.wait("scalar", "st", stv)
                        off = oc * 1024 + dh * 512
                        P.emit("scalar", lambda e, off=off, outp=outp: e.copy(
                            out=dstf[:, off:off + 512], in_=outp), inc="ss")
                        P.bank_set(bk, "ss")

            P.wait("tensor", "qS")
            P.wait("tensor", "sv")
            stage_AB(PTf, Wr_b, Cb, Wi_b, Sb)
            P.wait("vector", "ss")
            P.emit("vector", lambda e: e.tensor_scalar(
                out=arb(12288, 4096), in0=arb(12288, 4096), scalar1=-1.0,
                scalar2=None, op0=ALU.mult), inc="sv")
            P.wait("tensor", "sv")
            stage_AB(QnTf, Wi_b, Cb, Wr_b, Sb)
            P.wait("tensor", "ss")

            for dc in range(8):
                for mh in range(2):
                    bk = 2 + (2 * dc + mh) % 4
                    P.bank_wait(bk)
                    outp = pb[bk][:, :]
                    for oc in range(8):
                        t_mm(outp, PTb[:, oc, dc * 128:(dc + 1) * 128],
                             Cb[:, oc, mh * 512:mh * 512 + 512], oc == 0, False)
                    for oc in range(8):
                        t_mm(outp, QnTb[:, oc, dc * 128:(dc + 1) * 128],
                             Sb[:, oc, mh * 512:mh * 512 + 512], False, oc == 7,
                             inc="st" if oc == 7 else None)
                    stv = P.val("st")
                    P.wait("scalar", "st", stv)
                    off = dc * 1024 + mh * 512
                    P.emit("scalar", lambda e, off=off, outp=outp: e.activation(
                        out=Mhif[:, off:off + 512], in_=outp, func=AF.Copy,
                        scale=csts[:, 1:2]), inc="ss")
                    ssv = P.val("ss")
                    P.wait("vector", "ss", ssv)
                    P.emit("vector", lambda e, off=off, outp=outp:
                           e.scalar_tensor_tensor(
                               out=Mlof[:, off:off + 512], in0=outp,
                               scalar=0.5 / D, in1=Mhif[:, off:off + 512],
                               op0=ALU.mult, op1=ALU.subtract), inc="sv")
                    P.bank_set(bk, "sv")
            P.wait("sync", "sv")
            P.dma("sync", scr_M[:, 0:8192], arb(8192, 4096))
            P.dma("sync", scr_M[:, 8192:16384], arb(12288, 4096))
            P.wait("sync", "qS")

        def ffn_apply(xsrc=None):
            xs = scr_x3 if xsrc is None else xsrc
            Mhi = arb(0, 4096).rearrange("p (c m) -> p c m", m=1024)
            Mlo = arb(4096, 4096).rearrange("p (c m) -> p c m", m=1024)
            xfull = arf(8192, 16384).rearrange("p (c t) -> p c t", t=T)
            P.wait("sync", "sv")
            P.wait("sync", "ss")
            P.wait("sync", "st")
            P.dma("sync", arb(0, 4096), scr_M[:, 0:8192])
            P.dma("sync", arb(4096, 4096), scr_M[:, 8192:16384])
            for blk in range(NBLK):
                P.dma("sync", xfull[:, :, blk * BLK:(blk + 1) * BLK],
                      xs[:, :, blk * BLK:(blk + 1) * BLK])
            qv = P.val("qS")
            P.wait("tensor", "qS", qv)
            for mc in range(8):
                for blk in range(NBLK):
                    P.bank_wait(2 + blk)
                first = True
                for dc in range(8):
                    for Mv in (Mhi, Mlo):
                        for blk in range(NBLK):
                            t_mm(pb[2 + blk][:, :],
                                 Mv[:, dc, mc * 128:(mc + 1) * 128],
                                 xnbf[:, dc, blk * BLK:(blk + 1) * BLK],
                                 first, (dc == 7 and Mv is Mlo),
                                 inc="st" if (dc == 7 and Mv is Mlo and blk == 3)
                                 else None)
                        first = False
                stv = P.val("st")
                P.wait("vector", "st", stv)
                for blk in range(NBLK):
                    P.emit("vector", lambda e, mc=mc, blk=blk:
                           e.scalar_tensor_tensor(
                               out=xfull[:, mc, blk * BLK:(blk + 1) * BLK],
                               in0=pb[2 + blk][:, :],
                               scalar=vecs[:, 48 + mc:49 + mc],
                               in1=xfull[:, mc, blk * BLK:(blk + 1) * BLK],
                               op0=ALU.add, op1=ALU.add), inc="sv")
                    P.bank_set(2 + blk, "sv")
            P.wait("sync", "sv")
            for blk in range(NBLK):
                P.dma("sync", scr_x3[:, :, blk * BLK:(blk + 1) * BLK],
                      xfull[:, :, blk * BLK:(blk + 1) * BLK])
            P.wait("sync", "qS")

        def rwkv_apply(out_dst=None):
            xdst = scr_x3 if out_dst is None else out_dst
            Wr_v = arb(0, 4096).rearrange("p (c n) -> p c n", n=1024)
            Wk_v = arb(4096, 4096).rearrange("p (c n) -> p c n", n=1024)
            Wv_v = arb(8192, 4096).rearrange("p (c n) -> p c n", n=1024)
            xnb = arf(12288, 4104).rearrange("p (c n) -> p c n", n=513)
            xrb = arb(16392, 2048).rearrange("p (c n) -> p c n", n=BLK)
            xkb = arb(18440, 2048).rearrange("p (c n) -> p c n", n=BLK)
            rblk = arb(20488, 2048).rearrange("p (c n) -> p c n", n=BLK)
            kab = xrb
            krelu = tmp2
            vblk = xkb
            P.wait("gpsimd", "sv")
            P.wait("gpsimd", "ss")
            P.wait("gpsimd", "st")
            P.dma("gpsimd", Wr_v, W1_WrT.rearrange("(c p) n -> p c n", p=128),
                  inc="qG")
            P.dma("gpsimd", Wk_v, W1_WkT.rearrange("(c p) n -> p c n", p=128),
                  inc="qG")
            P.dma("gpsimd", Wv_v, W1_WvT.rearrange("(c p) n -> p c n", p=128),
                  inc="qG")
            ldg = P.val("qG")
            P.emit("vector", lambda e: e.memset(xnb[:, :, 0:1], 0.0), inc="sv")
            for blk in range(NBLK):
                sl = slice(blk * BLK, blk * BLK + BLK)
                P.wait("sync", "sv")
                P.wait("sync", "ss")
                P.dma("sync", xblk[:, :, :], scr_x3[:, :, sl])
                qv = P.val("qS")
                P.wait("tensor", "qS", qv)
                P.bank_wait(2)
                P.bank_wait(3)
                t_mm(pb[2][:, 0:BLK], ones_f1[:, :], rows[0:1, 0, sl],
                     True, True)
                t_mm(pb[3][:, 0:BLK], ones_f1[:, :], rows[0:1, 1, sl],
                     True, True, inc="st")
                stv = P.val("st")
                P.wait("scalar", "st", stv)
                P.emit("scalar", lambda e: e.copy(out=b1f[:, :], in_=pb[2][:, 0:BLK]))
                P.emit("scalar", lambda e: e.copy(out=b2f[:, :], in_=pb[3][:, 0:BLK]),
                       inc="ss")
                P.bank_set(2, "ss")
                P.bank_set(3, "ss")
                ssv = P.val("ss")
                P.wait("vector", "ss", ssv)
                P.wait("vector", "qS", qv)
                b1v2 = b1f[:, :].unsqueeze(1).broadcast_to((128, 2, BLK))
                b2v2 = b2f[:, :].unsqueeze(1).broadcast_to((128, 2, BLK))
                for g in range(4):
                    gsl = slice(2 * g, 2 * g + 2)
                    P.emit("vector", lambda e, gsl=gsl, b1v2=b1v2: e.tensor_tensor(
                        out=tmp2[:, :, :], in0=xblk[:, gsl, :], in1=b1v2,
                        op=ALU.mult), inc="sv")
                    P.fence("vector")
                    P.emit("vector", lambda e, gsl=gsl, b2v2=b2v2: e.tensor_tensor(
                        out=xnb[:, gsl, 1:513], in0=tmp2[:, :, :], in1=b2v2,
                        op=ALU.subtract), inc="sv")
                    P.fence("vector")
                for g in range(4):
                    gsl = slice(2 * g, 2 * g + 2)
                    P.emit("vector", lambda e, gsl=gsl: e.tensor_tensor(
                        out=tmp2[:, :, :], in0=xnb[:, gsl, 1:513],
                        in1=xnb[:, gsl, 0:512], op=ALU.subtract), inc="sv")
                    P.fence("vector")
                    for ci in range(2):
                        c = 2 * g + ci
                        P.emit("vector", lambda e, c=c, ci=ci: e.scalar_tensor_tensor(
                            out=xrb[:, c, :], in0=tmp2[:, ci, :],
                            scalar=murc[:, c:c + 1], in1=xnb[:, c, 0:512],
                            op0=ALU.mult, op1=ALU.add), inc="sv")
                        P.emit("vector", lambda e, c=c, ci=ci: e.scalar_tensor_tensor(
                            out=xkb[:, c, :], in0=tmp2[:, ci, :],
                            scalar=mukc[:, c:c + 1], in1=xnb[:, c, 0:512],
                            op0=ALU.mult, op1=ALU.add), inc="sv")
                    P.fence("vector")
                P.emit("vector", lambda e: e.tensor_copy(
                    out=xnb[:, :, 0:1], in_=xnb[:, :, 512:513]), inc="sv")
                P.fence("vector")
                svv = P.val("sv")
                P.wait("tensor", "sv", svv)
                P.wait("tensor", "qG", ldg)
                for mc in range(8):
                    bk = 2 + (mc % 4)
                    P.bank_wait(bk)
                    outp = pb[bk][:, :]
                    for kc in range(8):
                        t_mm(outp, Wr_v[:, kc, mc * 128:(mc + 1) * 128],
                             xrb[:, kc, :], kc == 0, kc == 7,
                             inc="st" if kc == 7 else None)
                    stv = P.val("st")
                    P.wait("scalar", "st", stv)
                    P.emit("scalar", lambda e, mc=mc, outp=outp: e.activation(
                        out=rblk[:, mc, :], in_=outp, func=AF.Sigmoid),
                        inc="ss")
                    P.bank_set(bk, "ss")
                for mc in range(8):
                    bk = 2 + (mc % 4)
                    P.bank_wait(bk)
                    outp = pb[bk][:, :]
                    for kc in range(8):
                        t_mm(outp, Wk_v[:, kc, mc * 128:(mc + 1) * 128],
                             xkb[:, kc, :], kc == 0, kc == 7,
                             inc="st" if kc == 7 else None)
                    stv = P.val("st")
                    P.wait("scalar", "st", stv)
                    P.emit("scalar", lambda e, mc=mc, outp=outp: e.activation(
                        out=krelu[:, mc % 2, :], in_=outp, func=AF.Relu),
                        inc="ss")
                    P.fence("scalar")
                    P.emit("scalar", lambda e, mc=mc: e.activation(
                        out=kab[:, mc, :], in_=krelu[:, mc % 2, :],
                        func=AF.Square), inc="ss")
                    P.bank_set(bk, "ss")
                P.wait("tensor", "ss")
                for mc in range(8):
                    bk = 2 + (mc % 4)
                    P.bank_wait(bk)
                    outp = pb[bk][:, :]
                    for kc in range(8):
                        t_mm(outp, Wv_v[:, kc, mc * 128:(mc + 1) * 128],
                             kab[:, kc, :], kc == 0, kc == 7,
                             inc="st" if kc == 7 else None)
                    stv = P.val("st")
                    P.wait("vector", "st", stv)
                    P.emit("vector", lambda e, mc=mc, outp=outp: e.tensor_tensor(
                        out=tmp2[:, mc % 2, :], in0=rblk[:, mc, :], in1=outp,
                        op=ALU.mult), inc="sv")
                    P.fence("vector")
                    P.emit("vector", lambda e, mc=mc: e.scalar_tensor_tensor(
                        out=xblk[:, mc, :], in0=tmp2[:, mc % 2, :], scalar=0.5,
                        in1=xblk[:, mc, :], op0=ALU.mult, op1=ALU.add),
                        inc="sv")
                    P.bank_set(bk, "sv")
                P.wait("sync", "sv")
                P.dma("sync", xdst[:, :, sl], xblk[:, :, :])
                P.wait("tensor", "sv")

        def swin_apply():
            wqkv = arb(0, 12288).rearrange("p (c n) -> p c n", n=3072)
            wproj = arb(12288, 4096).rearrange("p (c n) -> p c n", n=1024)
            qtile = arb(16384, 1024).rearrange("p (c n) -> p c n", n=256)
            ktile = arb(17408, 1024).rearrange("p (c n) -> p c n", n=256)
            vtile = arb(18432, 1024).rearrange("p (c n) -> p c n", n=1024)
            attno = arb(19456, 1024).rearrange("p (c n) -> p c n", n=256)
            pout = arf(20480, 4096).rearrange("p (c n) -> p c n", n=BLK)
            P.wait("gpsimd", "sv")
            P.wait("gpsimd", "ss")
            P.wait("gpsimd", "st")
            P.dma("gpsimd", wqkv, Q_wqkv.rearrange("(c p) n -> p c n", p=128),
                  inc="qG")
            P.dma("gpsimd", wproj, Q_wproj.rearrange("(c p) n -> p c n", p=128),
                  inc="qG")
            ldg = P.val("qG")
            P.wait("tensor", "qG", ldg)
            for pair in range(8):
                ts2 = slice(pair * 256, pair * 256 + 256)
                blk = pair // 2
                for part, wof, dst, scl_, bof in (
                        (0, 0, qtile, csts[:, 2:3], 0),
                        (1, 1024, ktile, None, 8)):
                    for oc in range(8):
                        bk = 4 + (oc % 2)
                        P.bank_wait(bk)
                        outp = pb[bk][:, 0:256]
                        for kc in range(8):
                            t_mm(outp, wqkv[:, kc, wof + oc * 128:wof + (oc + 1) * 128],
                                 xnbf[:, kc, ts2], kc == 0, kc == 7,
                                 inc="st" if kc == 7 else None)
                        stv = P.val("st")
                        P.wait("scalar", "st", stv)
                        if scl_ is None:
                            P.emit("scalar", lambda e, oc=oc, dst=dst, bof=bof,
                                   outp=outp: e.activation(
                                       out=dst[:, oc, :], in_=outp,
                                       func=AF.Identity,
                                       bias=qb_sc[:, bof + oc:bof + oc + 1]),
                                   inc="ss")
                        else:
                            P.emit("scalar", lambda e, oc=oc, dst=dst, bof=bof,
                                   outp=outp, scl_=scl_: e.activation(
                                       out=dst[:, oc, :], in_=outp,
                                       func=AF.Identity, scale=scl_,
                                       bias=qb_sc[:, bof + oc:bof + oc + 1]),
                                   inc="ss")
                        P.bank_set(bk, "ss")
                for ti in range(2):
                    ts = slice(pair * 256 + ti * 128, pair * 256 + ti * 128 + 128)
                    for vb in range(2):
                        bk = 4 + (2 * ti + vb) % 3
                        P.bank_wait(bk)
                        outp = pb[bk][:, :]
                        for kc in range(8):
                            t_mm(outp, xnbf[:, kc, ts],
                                 wqkv[:, kc, 2048 + vb * 512:2048 + (vb + 1) * 512],
                                 kc == 0, False)
                        t_mm(outp, onesb_1[:, :],
                             qbv_row[0:1, vb * 512:(vb + 1) * 512],
                             False, True, inc="st")
                        stv = P.val("st")
                        P.wait("scalar", "st", stv)
                        P.emit("scalar", lambda e, ti=ti, vb=vb, outp=outp: e.copy(
                            out=vtile[:, ti, vb * 512:(vb + 1) * 512],
                            in_=outp), inc="ss")
                        P.bank_set(bk, "ss")
                P.wait("tensor", "ss")
                for ti in range(2):
                    P.wait("tensor", "ss")
                    P.wait("vector", "st")
                    P.wait("scalar", "st")
                    sc_st = {}
                    mx_sv = {}
                    ex_ss = {}
                    sr_sv = {}
                    ts_sv = {}
                    tr_st = {}
                    pt_ss = {}
                    pv_st = {}

                    def st_scores(h, ti=ti):
                        bk = 4 + (h % 3)
                        P.bank_wait(bk)
                        hc, hp = h // 2, (h % 2) * 64
                        qs = qtile[hp:hp + 64, hc, ti * 128:ti * 128 + 128]
                        ks = ktile[hp:hp + 64, hc, ti * 128:ti * 128 + 128]
                        t_mm(pb[bk][:, 0:128], qs, ks, True, False)
                        t_mm(pb[bk][:, 0:128], e8_sb[:, :],
                             bcol[:, h, :, :].rearrange("a w b -> a (w b)"),
                             False, False)
                        t_mm(pb[bk][:, 0:128], e16_sb[:, :], bkw_sb[:, :],
                             False, True, inc="st")
                        sc_st[h] = P.val("st")

                    def sv_max(h):
                        bk = 4 + (h % 3)
                        P.wait("vector", "st", sc_st[h])
                        P.wait("vector", "ss", ex_ss.get(h - 2, 0))
                        P.emit("vector", lambda e, h=h, bk=bk: e.reduce_max(
                            out=vecs[:, 56 + h % 2:57 + h % 2],
                            in_=pb[bk][:, 0:128],
                            axis=mybir.AxisListType.X, negate=True), inc="sv")
                        mx_sv[h] = P.val("sv")

                    def ss_exp(h):
                        bk = 4 + (h % 3)
                        P.wait("scalar", "sv", mx_sv[h])
                        P.wait("scalar", "sv", ts_sv.get(h - 2, 0))
                        P.emit("scalar", lambda e, h=h, bk=bk: e.activation(
                            out=PbfB[h % 2][:, :], in_=pb[bk][:, 0:128],
                            func=AF.Exp,
                            bias=vecs[:, 56 + h % 2:57 + h % 2]), inc="ss")
                        ex_ss[h] = P.val("ss")
                        P.bank_set(bk, "ss")

                    def sv_sum(h):
                        P.wait("vector", "ss", ex_ss[h])
                        P.wait("vector", "sv", ts_sv.get(h - 2, 0))
                        P.emit("vector", lambda e, h=h: e.reduce_sum(
                            out=vecs[:, 58 + h % 2:59 + h % 2],
                            in_=PbfB[h % 2][:, :],
                            axis=mybir.AxisListType.X), inc="sv")
                        sr_sv[h] = P.val("sv")

                    def sv_recip(h):
                        P.wait("vector", "sv", sr_sv[h])
                        P.emit("vector", lambda e, h=h: e.reciprocal(
                            out=vecs[:, 58 + h % 2:59 + h % 2],
                            in_=vecs[:, 58 + h % 2:59 + h % 2]), inc="sv")
                        sr_sv[h] = P.val("sv")

                    def sv_scale(h):
                        P.wait("vector", "sv", sr_sv[h])
                        P.wait("vector", "st", tr_st.get(h - 2, 0))
                        P.emit("vector", lambda e, h=h: e.tensor_scalar(
                            out=PbfPB[h % 2][:, :], in0=PbfB[h % 2][:, :],
                            scalar1=vecs[:, 58 + h % 2:59 + h % 2], scalar2=None,
                            op0=ALU.mult), inc="sv")
                        ts_sv[h] = P.val("sv")

                    def st_transpose(h):
                        P.wait("tensor", "sv", ts_sv[h])
                        if h >= 2:
                            P.wait("tensor", "ss", pt_ss.get(h - 2, 0))
                        P.emit("tensor", lambda e, h=h: e.transpose(
                            out=ptrb[:, (h % 2) * 128:(h % 2) * 128 + 128],
                            in_=PbfPB[h % 2][:, :], identity=idn_sb[:, :]),
                            inc="st")
                        tr_st[h] = P.val("st")

                    def ss_ptevac(h):
                        P.wait("scalar", "st", tr_st[h])
                        P.wait("scalar", "st", pv_st.get(h - 2, 0))
                        P.emit("scalar", lambda e, h=h: e.copy(
                            out=PtB[h % 2][:, :],
                            in_=ptrb[:, (h % 2) * 128:(h % 2) * 128 + 128]),
                            inc="ss")
                        pt_ss[h] = P.val("ss")

                    def st_pv(h, ti=ti):
                        P.wait("tensor", "ss", pt_ss[h])
                        if h % 2 == 0:
                            P.bank_wait(2)
                        hp = (h % 2) * 64
                        t_mm(pb[2][hp:hp + 64, 0:128],
                             vtile[:, ti, h * 64:(h + 1) * 64],
                             PtB[h % 2][:, :], True, True, inc="st")
                        pv_st[h] = P.val("st")
                        if h % 2 == 1:
                            P.wait("scalar", "st", pv_st[h])
                            P.emit("scalar", lambda e, h=h, ti=ti: e.copy(
                                out=attno[:, h // 2, ti * 128:ti * 128 + 128],
                                in_=pb[2][:, 0:128]), inc="ss")
                            P.bank_set(2, "ss")

                    for h in range(16 + 4):
                        if 0 <= h - 4 < 16:
                            st_pv(h - 4)
                        if 0 <= h - 3 < 16:
                            sv_recip(h - 3)
                            sv_scale(h - 3)
                            st_transpose(h - 3)
                            ss_ptevac(h - 3)
                        if 0 <= h - 2 < 16:
                            sv_sum(h - 2)
                        if h < 16:
                            st_scores(h)
                        if 0 <= h - 1 < 16:
                            sv_max(h - 1)
                            ss_exp(h - 1)
                P.wait("tensor", "ss")
                for oc in range(8):
                    bk = 4 + (oc % 2)
                    P.bank_wait(bk)
                    outp = pb[bk][:, 0:256]
                    for kc in range(8):
                        t_mm(outp, wproj[:, kc, oc * 128:(oc + 1) * 128],
                             attno[:, kc, :], kc == 0, kc == 7,
                             inc="st" if kc == 7 else None)
                    stv = P.val("st")
                    P.wait("scalar", "st", stv)
                    P.emit("scalar", lambda e, oc=oc, pair=pair, outp=outp: e.copy(
                        out=pout[:, oc, (pair % 2) * 256:(pair % 2) * 256 + 256],
                        in_=outp), inc="ss")
                    P.bank_set(bk, "ss")
                if pair % 2 == 1:
                    P.wait("sync", "ss")
                    P.dma("sync", scr_po3[:, :, blk * BLK:blk * BLK + BLK],
                          pout[:, :, :])
                    P.wait("scalar", "qS")
            pfB = [pout, arf(4096, 4096).rearrange("p (c n) -> p c n", n=BLK)]
            ep_sv = {}
            eqv = {}

            def ep_dma(blk):
                if blk < 2:
                    P.wait("sync", "sv")
                    P.wait("sync", "ss")
                    P.wait("sync", "st")
                else:
                    P.wait("sync", "sv", ep_sv[blk - 2])
                P.dma("sync", xbB[blk % 2][:, :, :],
                      scr_x3[:, :, blk * BLK:(blk + 1) * BLK])
                pf = pfB[blk % 2]
                if blk == 0:
                    P.dma("sync", pf[:, :, 0:4], scr_po3[:, :, T - 4:T])
                    P.dma("sync", pf[:, :, 4:512], scr_po3[:, :, 0:508])
                else:
                    P.dma("sync", pf[:, :, :],
                          scr_po3[:, :, blk * BLK - 4:blk * BLK + 508])
                eqv[blk] = P.val("qS")

            ep_dma(0)
            ep_dma(1)
            for blk in range(NBLK):
                sl = slice(blk * BLK, blk * BLK + BLK)
                xb = xbB[blk % 2]
                pf = pfB[blk % 2]
                P.wait("vector", "qS", eqv[blk])
                for c in range(8):
                    P.emit("vector", lambda e, c=c, xb=xb, pf=pf:
                           e.scalar_tensor_tensor(
                               out=xb[:, c, :], in0=pf[:, c, :],
                               scalar=pbias[:, c:c + 1], in1=xb[:, c, :],
                               op0=ALU.add, op1=ALU.add), inc="sv")
                ep_sv[blk] = P.val("sv")
                P.wait("sync", "sv", ep_sv[blk])
                P.dma("sync", scr_x3[:, :, sl], xb[:, :, :])
                if blk + 2 < NBLK:
                    ep_dma(blk + 2)


        def mamba_apply():
            inw = arb(0, 8192).rearrange("p (c n) -> p c n", n=2048)
            convinB = [arf(8192, 2052), arf(10244, 2052)]
            convaccB = [arf(12296, 2048), arf(14344, 2048)]
            x1tB = [arb(16392, 1024), arb(17416, 1024)]
            dbc = arb(23552, 1024)
            scr_x13 = scr_x1.rearrange("(c p) t -> p c t", p=128)
            scr_yg3 = scr_yg.rearrange("(c p) t -> p c t", p=128)
            HCV = T // 2

            P.wait("vector", "ss")
            P.emit("vector", lambda e: e.memset(hcar[:, :, :], 0.0))
            P.emit("vector", lambda e: e.memset(convinB[0][:, 0:3], 0.0))
            P.emit("vector", lambda e: e.memset(convinB[1][:, 0:3], 0.0),
                   inc="sv")
            P.wait("gpsimd", "sv")
            P.wait("gpsimd", "ss")
            P.wait("gpsimd", "st")
            P.dma("gpsimd", inw, L0_inwT.rearrange("(c p) n -> p c n", p=128),
                  inc="qG")
            ldg = P.val("qG")
            P.wait("tensor", "qG", ldg)
            P.wait("tensor", "sv")
            st_in = {}
            cv_end = {}
            si_ss = {}
            d1_st = {}
            sp_q = {}

            def m1_inproj(ET):
                cur = ET % 2
                banks = ((4, 5), (6, 4))
                first_evac = True
                for pr in range(2):
                    for ti in range(2):
                        bk = banks[pr][ti]
                        tb = 2 * pr + ti
                        P.bank_wait(bk)
                        for kc in range(8):
                            t_mm(pb[bk][:, :],
                                 inw[:, kc, ET * 128:(ET + 1) * 128],
                                 xnbf[:, kc, tb * BLK:(tb + 1) * BLK],
                                 kc == 0, kc == 7,
                                 inc="st" if kc == 7 else None)
                        stv = P.val("st")
                        P.wait("scalar", "st", stv)
                        if first_evac:
                            P.wait("scalar", "sv", cv_end.get(ET - 2, 0))
                            first_evac = False
                        P.emit("scalar", lambda e, tb=tb, bk=bk, cur=cur: e.copy(
                            out=convinB[cur][:, 3 + tb * BLK:3 + (tb + 1) * BLK],
                            in_=pb[bk][:, :]), inc="ss")
                        P.bank_set(bk, "ss")
                st_in[ET] = P.val("ss")

            def m1_conv(ET):
                cur = ET % 2
                convin = convinB[cur]
                convacc = convaccB[cur]
                P.wait("vector", "ss", st_in[ET])
                cv_sv = {}
                for hh in range(2):
                    P.emit("vector", lambda e, ET=ET, hh=hh, convin=convin,
                           convacc=convacc: e.tensor_scalar(
                        out=convacc[:, hh * HCV:(hh + 1) * HCV],
                        in0=convin[:, hh * HCV:hh * HCV + HCV],
                        scalar1=convw_sb[:, ET, 0:1], scalar2=vecs[:, ET:ET + 1],
                        op0=ALU.mult, op1=ALU.add), inc="sv")
                    cv_sv[(0, hh)] = P.val("sv")
                for k in range(1, 4):
                    for hh in range(2):
                        P.wait("vector", "sv", cv_sv[(k - 1, hh)])
                        P.emit("vector", lambda e, ET=ET, k=k, hh=hh,
                               convin=convin, convacc=convacc:
                               e.scalar_tensor_tensor(
                                   out=convacc[:, hh * HCV:(hh + 1) * HCV],
                                   in0=convin[:, k + hh * HCV:k + hh * HCV + HCV],
                                   scalar=convw_sb[:, ET, k:k + 1],
                                   in1=convacc[:, hh * HCV:(hh + 1) * HCV],
                                   op0=ALU.mult, op1=ALU.add), inc="sv")
                        cv_sv[(k, hh)] = P.val("sv")
                P.emit("vector", lambda e: e.engine_nop(), inc="sv")
                cv_end[ET] = P.val("sv")

            def m1_silu(ET):
                cur = ET % 2
                P.wait("scalar", "sv", cv_end[ET])
                P.wait("scalar", "st", d1_st.get(ET - 2, 0))
                P.wait("scalar", "qS", sp_q.get(ET - 2, 0))
                P.emit("scalar", lambda e, cur=cur: e.activation(
                    out=x1tB[cur][:, :], in_=convaccB[cur][:, :], func=AF.Silu),
                    inc="ss")
                si_ss[ET] = P.val("ss")

            def m1_spill(ET):
                P.wait("sync", "ss", si_ss[ET])
                P.dma("sync", scr_x13[:, :, :][0:128, ET, :], x1tB[ET % 2][:, :])
                sp_q[ET] = P.val("qS")

            def m1_d1bc(ET):
                P.wait("tensor", "ss", si_ss[ET])
                for tb in range(NBLK):
                    t_mm(pb[tb][0:96, :], d1bc_sb[:, ET, :],
                         x1tB[ET % 2][:, tb * BLK:(tb + 1) * BLK],
                         ET == 0, ET == 15,
                         inc="st" if tb == 3 else None)
                d1_st[ET] = P.val("st")

            for ET in range(16):
                m1_inproj(ET)
                if ET >= 1:
                    m1_conv(ET - 1)
                    m1_silu(ET - 1)
                    m1_spill(ET - 1)
                    m1_d1bc(ET - 1)
            m1_conv(15)
            m1_silu(15)
            m1_spill(15)
            m1_d1bc(15)
            P.wait("scalar", "st", d1_st[15])
            for tb in range(NBLK):
                P.emit("scalar", lambda e, tb=tb: e.copy(
                    out=dbc[0:96, tb * BLK:(tb + 1) * BLK], in_=pb[tb][0:96, :]),
                    inc="ss")
            P.wait("sync", "ss")
            P.barrier()

            Brep = arb(0, 8192)
            Crep = arb(8192, 8192)
            delta = arf(16384, 1024)
            AbarB = [arf(17408, 1024), arf(18432, 1024)]
            dxb = arb(19456, 512)
            bbufB = [arb(19968, 512), arb(20480, 512)]
            hbufB = [arb(20992, 512), arb(23040, 512)]
            zbufB = [arb(21504, 512), arb(22016, 512)]
            x1t2 = arb(22528, 512)
            HT = 1024
            xfl = xblk.rearrange("p c b -> p (c b)")
            deltaB2 = [delta, xfl[:, 0:1024]]
            x1t2B = [x1t2, xfl[:, 1024:1536].bitcast(BF16)]
            dxbB = [dxb, xfl[:, 1536:2048].bitcast(BF16)]
            yraw = xfl[:, 2048:2560].bitcast(BF16)
            pl = {}
            for half in range(2):
                hs = slice(half * HT, half * HT + HT)
                P.wait("sync", "st")
                P.dma("sync", bmt[:, :], dbc[64:80, hs])
                P.dma("sync", cmt[:, :], dbc[80:96, hs])
                bcq = P.val("qS")
                P.wait("tensor", "qS", bcq)
                for n in range(16):
                    for si, (src, dst) in enumerate(((bmt, Brep), (cmt, Crep))):
                        for b2 in range(2):
                            bk = 2 + b2
                            P.bank_wait(bk)
                            bank = pb[bk]
                            t_mm(bank[:, :], selr_sb[:, n, :],
                                 src[:, b2 * 512:(b2 + 1) * 512],
                                 True, True, inc="st")
                            stv = P.val("st")
                            P.wait("scalar", "st", stv)
                            P.emit("scalar", lambda e, n=n, dst=dst, b2=b2,
                                   bank=bank: e.copy(
                                       out=dst[:, n * 1024 + b2 * 512:
                                               n * 1024 + (b2 + 1) * 512],
                                       in_=bank[:, :]), inc="ss")
                            P.bank_set(bk, "ss")
                def prep_D(ET1, half=half, hs=hs):
                    P.wait("sync", "sv")
                    P.dma("sync", x1t2B[ET1 % 2][:, :], scr_x13[0:128, ET1, hs])
                    return P.val("qS")

                def prep_M(ET1, half=half):
                    P.bank_wait(4)
                    P.bank_wait(5)
                    for b2 in range(2):
                        t_mm(pb[4 + b2][:, :], wd2_sb[:, ET1, :],
                             dbc[0:64, half * HT + b2 * 512:
                                 half * HT + (b2 + 1) * 512],
                             True, True, inc="st" if b2 == 1 else None)
                    return P.val("st")

                def prep_E(ET1, stv, ss_guard, sv_guard):
                    j = ET1 % 2
                    P.wait("scalar", "st", stv)
                    P.wait("scalar", "ss", ss_guard)
                    P.wait("scalar", "sv", sv_guard)
                    for b2 in range(2):
                        P.emit("scalar", lambda e, ET1=ET1, b2=b2, j=j:
                               e.activation(
                                   out=deltaB2[j][:, b2 * 512:(b2 + 1) * 512],
                                   in_=pb[4 + b2][:, :], func=AF.Exp,
                                   bias=vecs[:, 16 + ET1:17 + ET1]), inc="ss")
                    P.bank_set(4, "ss")
                    P.bank_set(5, "ss")
                    P.fence("scalar")
                    P.emit("scalar", lambda e, j=j: e.activation(
                        out=deltaB2[j][:, :], in_=deltaB2[j][:, :],
                        func=AF.Ln, bias=1.0), inc="ss")
                    return P.val("ss")

                def prep_X(ET1, qv, ssv, war_sv):
                    j = ET1 % 2
                    P.wait("vector", "ss", ssv)
                    P.wait("vector", "qS", qv)
                    P.wait("vector", "sv", war_sv)
                    P.emit("vector", lambda e, j=j: e.tensor_tensor(
                        out=dxbB[j][:, :], in0=deltaB2[j][:, :],
                        in1=x1t2B[j][:, :], op=ALU.mult), inc="sv")
                    return P.val("sv")

                pl["exp_end"] = P.val("ss")
                pl["b15"] = P.val("sv")
                for ET in range(16):
                    cur = ET % 2
                    if ET == 0:
                        qv = prep_D(0)
                        stv0 = prep_M(0)
                        ssv = prep_E(0, stv0, P.val("ss"), P.val("sv"))
                        dxv = prep_X(0, qv, ssv, P.val("sv"))
                        pl["dxv_prev"] = dxv
                    else:
                        qv, ssv, dxv = pl["qv"], pl["ssv"], pl["dxv"]
                        pl["dxv_prev"] = dxv
                    sc14 = pl.get("scan14", dxv)
                    sc15 = pl.get("scan15", dxv)
                    scan_sv = {}
                    b_sv = {}
                    z_sv = {}
                    ymm_st = {}

                    def emit_exp(n, ET=ET, cur=cur, ssv=ssv, sc14=sc14,
                                 sc15=sc15, dxv=dxv):
                        if n >= 2:
                            P.wait("scalar", "sv", scan_sv[n - 2])
                        else:
                            P.wait("scalar", "sv", max(sc14 if n == 0 else sc15,
                                                       dxv))
                            P.wait("scalar", "ss", ssv)
                        P.emit("scalar", lambda e, ET=ET, n=n: e.activation(
                            out=AbarB[n % 2][:, :], in_=deltaB2[cur][:, :],
                            func=AF.Exp, scale=Aneg[:, ET, n:n + 1]), inc="ss")
                        return P.val("ss")

                    def emit_b(n, cur=cur, sc14=sc14, sc15=sc15, dxv=dxv):
                        if n >= 2:
                            P.wait("vector", "sv", scan_sv[n - 2])
                        else:
                            P.wait("vector", "sv", max(sc14 if n == 0 else sc15,
                                                       dxv))
                        P.emit("vector", lambda e, n=n, cur=cur: e.tensor_tensor(
                            out=bbufB[n % 2][:, :], in0=dxbB[cur][:, :],
                            in1=Brep[:, n * 1024:(n + 1) * 1024], op=ALU.mult),
                            inc="sv")
                        b_sv[n] = P.val("sv")

                    def emit_scan(n, exp_ss, ET=ET, half=half):
                        P.wait("vector", "ss", exp_ss)
                        P.wait("vector", "sv", b_sv[n])
                        if n >= 2:
                            P.wait("vector", "st", ymm_st[n - 2])
                        P.emit("vector", lambda e, ET=ET, n=n: e.tensor_tensor_scan(
                            out=hbufB[n % 2][:, :], data0=AbarB[n % 2][:, :],
                            data1=bbufB[n % 2][:, :],
                            initial=hcar[:, ET, n:n + 1],
                            op0=ALU.mult, op1=ALU.add), inc="sv")
                        scan_sv[n] = P.val("sv")

                    def emit_z(n, ET=ET, half=half):
                        P.wait("vector", "sv", scan_sv[n])
                        P.emit("vector", lambda e, n=n: e.tensor_tensor(
                            out=zbufB[n % 2][:, :], in0=hbufB[n % 2][:, :],
                            in1=Crep[:, n * 1024:(n + 1) * 1024], op=ALU.mult),
                            inc="sv")
                        z_sv[n] = P.val("sv")
                        if half == 0:
                            P.emit("vector", lambda e, ET=ET, n=n: e.tensor_copy(
                                out=hcar[:, ET, n:n + 1],
                                in_=hbufB[n % 2][:, HT - 1:HT]))
                        P.wait("tensor", "sv", z_sv[n])
                        for b2 in range(2):
                            t_mm(pb[b2][:, :], idn_sb[:, :],
                                 zbufB[n % 2][:, b2 * 512:(b2 + 1) * 512],
                                 n == 0, n == 15,
                                 inc="st" if b2 == 1 else None)
                        ymm_st[n] = P.val("st")

                    exp_ss = {n: None for n in range(16)}
                    exp_ss[0] = emit_exp(0)
                    exp_ss[1] = emit_exp(1)
                    emit_b(0)
                    emit_b(1)
                    for n in range(16):
                        emit_scan(n, exp_ss[n])
                        if n + 2 < 16:
                            exp_ss[n + 2] = emit_exp(n + 2)
                            emit_b(n + 2)
                        emit_z(n)
                        if ET < 15:
                            if n == 1:
                                pl["qv"] = prep_D(ET + 1)
                            elif n == 2:
                                pl["stv"] = prep_M(ET + 1)
                            elif n == 5:
                                pl["ssv"] = prep_E(ET + 1, pl["stv"],
                                                   pl["exp_end"],
                                                   pl["dxv_prev"])
                            elif n == 8:
                                pl["dxv"] = prep_X(ET + 1, pl["qv"],
                                                   pl["ssv"], pl["b15"])
                    P.wait("vector", "st", ymm_st[15])
                    P.wait("vector", "qS", pl.get("yq", 0))
                    for b2 in range(2):
                        P.emit("vector", lambda e, ET=ET, b2=b2, cur=cur:
                               e.scalar_tensor_tensor(
                                   out=yraw[:, b2 * 512:(b2 + 1) * 512],
                                   in0=x1t2B[cur][:, b2 * 512:(b2 + 1) * 512],
                                   scalar=vecs[:, 32 + ET:33 + ET],
                                   in1=pb[b2][:, :],
                                   op0=ALU.mult, op1=ALU.add), inc="sv")
                    P.wait("sync", "sv")
                    P.dma("sync", scr_yg3[0:128, ET, hs], yraw[:, :])
                    pl["yq"] = P.val("qS")
                    P.wait("tensor", "sv")
                    pl["scan14"] = scan_sv[14]
                    pl["scan15"] = scan_sv[15]
                    pl["exp_end"] = P.val("ss")
                    pl["b15"] = b_sv[15]
            skw = arb(0, 8192).rearrange("p (c n) -> p c n", n=2048)
            otw = arb(8192, 8192).rearrange("p (c n) -> p c n", n=1024)
            sskip = arb(16384, 4096).rearrange("p (c n) -> p c n", n=BLK)
            ygb = arb(20480, 4096).rearrange("p (c n) -> p c n", n=BLK)
            P.wait("gpsimd", "sv")
            P.wait("gpsimd", "ss")
            P.wait("gpsimd", "st")
            P.dma("gpsimd", skw, L0_skipT.rearrange("(c p) n -> p c n", p=128),
                  inc="qG")
            P.dma("gpsimd", otw, L0_outwT.rearrange("(c p) n -> p c n", p=128),
                  inc="qG")
            ldg = P.val("qG")
            P.wait("tensor", "qG", ldg)
            P.wait("sync", "ss")
            P.wait("sync", "st")
            for blk in range(NBLK):
                sl = slice(blk * BLK, blk * BLK + BLK)
                P.wait("sync", "sv")
                P.dma("sync", ygb[:, :, :],
                      scr_yg3[:, :, sl])
                P.dma("sync", xblk[:, :, :], scr_x3[:, :, sl])
                qv = P.val("qS")
                for ec in range(16):
                    bk = 2 + (ec % 4)
                    P.bank_wait(bk)
                    outp = pb[bk][:, :]
                    for kc in range(8):
                        t_mm(outp, skw[:, kc, ec * 128:(ec + 1) * 128],
                             xnbf[:, kc, sl], kc == 0, kc == 7,
                             inc="st" if kc == 7 else None)
                    stv = P.val("st")
                    P.wait("scalar", "st", stv)
                    P.emit("scalar", lambda e, ec=ec, outp=outp: e.activation(
                        out=sskip[:, ec, :], in_=outp, func=AF.Silu), inc="ss")
                    P.bank_set(bk, "ss")
                ssv = P.val("ss")
                P.wait("vector", "ss", ssv)
                P.wait("vector", "qS", qv)
                for hh in range(2):
                    hsl = slice(8 * hh, 8 * hh + 8)
                    P.emit("vector", lambda e, hsl=hsl: e.tensor_tensor(
                        out=ygb[:, hsl, :], in0=ygb[:, hsl, :],
                        in1=sskip[:, hsl, :], op=ALU.mult), inc="sv")
                svv = P.val("sv")
                P.wait("tensor", "sv", svv)
                for dc in range(8):
                    bk = dc % 2
                    P.bank_wait(bk)
                    outp = pb[bk][:, :]
                    for ec in range(16):
                        t_mm(outp, otw[:, ec, dc * 128:(dc + 1) * 128],
                             ygb[:, ec, :], ec == 0, ec == 15,
                             inc="st" if ec == 15 else None)
                    stv = P.val("st")
                    P.wait("vector", "st", stv)
                    P.emit("vector", lambda e, dc=dc, outp=outp: e.tensor_tensor(
                        out=xblk[:, dc, :], in0=outp, in1=xblk[:, dc, :],
                        op=ALU.add), inc="sv")
                    P.bank_set(bk, "sv")
                P.wait("sync", "sv")
                P.dma("sync", scr_x3[:, :, sl], xblk[:, :, :])
            P.wait("sync", "qS")

        last = max(ENABLE) if ENABLE else 0
        if 1 in ENABLE:
            ln_stats(xsrc=x_in3)
            xn_build(xsrc=x_in3)
            ffn_precompute()
            ffn_apply(xsrc=x_in3)

        if 2 in ENABLE:
            ln_stats(want_sl2=True)
            xn_build(0, 1, use_nw=True)
            P.wait("sync", "sv")
            P.dma("sync", rows[0:1, 0, :], st16[:, 4, :])
            P.dma("sync", rows[0:1, 1, :], st16[:, 5, :])
            xn_build(0, 1, xprime_only=True)
            mamba_apply()

        if 3 in ENABLE:
            ln_stats()
            xn_build()
            ffn_apply()

        if 4 in ENABLE:
            ln_stats()
            rwkv_apply()

        if 5 in ENABLE:
            ln_stats()
            xn_build(roll=True)
            swin_apply()

        if 6 in ENABLE:
            ln_stats()
            rwkv_apply(out_dst=out3 if last == 6 else None)

        if last != 6:
            for h in range(NBLK):
                sl = slice(h * BLK, h * BLK + BLK)
                P.dma("sync", xblk[:, :, :], scr_x3[:, :, sl])
                P.wait("sync", "qS")
                P.dma("sync", out3[:, :, sl], xblk[:, :, :])
                P.wait("sync", "qS")
        P.wait("sync", "qS")
        P.barrier()

        block = EE(nc.Block())

        @block.sync
        def _(eng):
            for fn in P.ops["sync"]:
                fn(eng)

        @block.gpsimd
        def _(eng):
            for fn in P.ops["gpsimd"]:
                fn(eng)

        @block.scalar
        def _(eng):
            for fn in P.ops["scalar"]:
                fn(eng)

        @block.vector
        def _(eng):
            for fn in P.ops["vector"]:
                fn(eng)

        @block.tensor
        def _(eng):
            for fn in P.ops["tensor"]:
                fn(eng)

    return nc


def host_prep(inputs):
    f32 = np.float32
    a = np.arange(D, dtype=np.float64)
    ang = 2 * np.pi / D * np.outer(a, a)
    Cb = np.cos(ang).astype(NPBF)
    Sb = np.sin(ang).astype(NPBF)

    WS = 8
    coords = np.stack(np.meshgrid(np.arange(WS), np.arange(WS), indexing="ij"))
    cf = coords.reshape(2, -1)
    rel = cf[:, :, None] - cf[:, None, :]
    rel = rel.transpose(1, 2, 0).astype(np.int64)
    rel[..., 0] += WS - 1
    rel[..., 1] += WS - 1
    rel[..., 0] *= 2 * WS - 1
    idx8 = rel.sum(-1)[:WS, :WS]
    OH = np.zeros((225, 64), NPBF)
    for aa in range(8):
        for b in range(8):
            OH[idx8[aa, b], aa * 8 + b] = 1.0
    E8 = np.zeros((8, 128), NPBF)
    for q in range(128):
        E8[q % 8, q] = 1.0
    E16 = np.zeros((16, 128), NPBF)
    for q in range(128):
        E16[q // 8, q] = 1.0
    BK = np.full((16, 128), -32768.0, NPBF)
    for w in range(16):
        BK[w, w * 8:w * 8 + 8] = 0.0
    IDN = np.eye(128, dtype=NPBF)
    SELRh = np.zeros((16, 16, 128), NPBF)
    for n in range(16):
        SELRh[n, n, :] = 1.0

    shared = {
        "CC": Cb, "SS": Sb,
        "FF_Wr": np.ascontiguousarray(inputs["L0_ffn_w_freq_real"], f32),
        "FF_Wi": np.ascontiguousarray(inputs["L0_ffn_w_freq_imag"], f32),
        "FF_bias": np.ascontiguousarray(inputs["L0_ffn_bias"], f32),
        "L0_inwT": np.ascontiguousarray(np.asarray(inputs["L0_attn_in_w"], f32).T),
        "L0_skipT": np.ascontiguousarray(np.asarray(inputs["L0_attn_skip_w"], f32).T),
        "L0_outwT": np.ascontiguousarray(np.asarray(inputs["L0_attn_out_w"], f32).T),
        "L0_d1bcT": np.ascontiguousarray(np.concatenate(
            [np.asarray(inputs["L0_attn_wd1"], f32).T,
             np.asarray(inputs["L0_attn_wb"], f32).T,
             np.asarray(inputs["L0_attn_wc"], f32).T], axis=1)),
        "L0_wd2T": np.ascontiguousarray(np.asarray(inputs["L0_attn_wd2"], f32).T),
        "L0_Alog": np.ascontiguousarray(inputs["L0_attn_A_log"], f32),
        "L0_convw": np.ascontiguousarray(inputs["L0_attn_conv_w"], f32),
        "L0_convb": np.ascontiguousarray(inputs["L0_attn_conv_b"], f32),
        "L0_wd2b": np.ascontiguousarray(inputs["L0_attn_wd2_b"], f32),
        "L0_WD": np.ascontiguousarray(inputs["L0_attn_W_D"], f32),
        "L0_normw": np.ascontiguousarray(inputs["L0_attn_norm_w"], f32),
        "Q_wqkv": np.ascontiguousarray(inputs["L1_attn_w_qkv"], f32),
        "Q_bqkv": np.ascontiguousarray(inputs["L1_attn_b_qkv"], f32),
        "Q_wproj": np.ascontiguousarray(inputs["L1_attn_w_proj"], f32),
        "Q_bproj": np.ascontiguousarray(inputs["L1_attn_b_proj"], f32),
        "Q_btab": np.ascontiguousarray(inputs["L1_attn_bias_table"], f32),
        "Q_OH": OH, "E8c": E8, "E16w": E16, "BKw": BK, "IDN": IDN,
        "SELR": SELRh.transpose(1, 0, 2) if False else SELRh,
        "W1_WrT": np.ascontiguousarray(np.asarray(inputs["L1_ffn_W_r"], f32).T),
        "W1_WkT": np.ascontiguousarray(np.asarray(inputs["L1_ffn_W_k"], f32).T),
        "W1_WvT": np.ascontiguousarray(np.asarray(inputs["L1_ffn_W_v"], f32).T),
        "VECC": np.ascontiguousarray(np.concatenate([
            np.asarray(inputs["L0_attn_conv_b"], f32).reshape(16, 128).T,
            np.asarray(inputs["L0_attn_wd2_b"], f32).reshape(16, 128).T,
            np.asarray(inputs["L0_attn_W_D"], f32).reshape(16, 128).T,
            np.asarray(inputs["L0_ffn_bias"], f32).reshape(8, 128).T,
            np.asarray(inputs["L1_attn_b_qkv"], f32).reshape(24, 128).T,
            np.asarray(inputs["L1_attn_b_proj"], f32).reshape(8, 128).T,
            np.asarray(inputs["L1_ffn_mu_r"], f32).reshape(8, 128).T,
            np.asarray(inputs["L1_ffn_mu_k"], f32).reshape(8, 128).T,
        ], axis=1)),
        "W1_mur": np.ascontiguousarray(inputs["L1_ffn_mu_r"], f32),
        "W1_muk": np.ascontiguousarray(inputs["L1_ffn_mu_k"], f32),
    }
    resid = np.asarray(inputs["L0_attn_resid"], f32)
    in_maps = []
    for c in range(NC):
        m = dict(shared)
        m["x_in"] = np.ascontiguousarray(resid[c].T)
        in_maps.append(m)
    return in_maps


def kernel(**inputs):
    in_maps = host_prep(inputs)
    nc = build_program()
    res = run_bass_kernel_spmd(nc, in_maps, core_ids=list(range(NC)))
    outs = [np.ascontiguousarray(np.asarray(r["out"]).T) for r in res.results]
    return np.stack(outs, axis=0)
